# version 13
# baseline (speedup 1.0000x reference)
import numpy as np
import concourse.bass as bass
import concourse.mybir as mybir
from concourse.bass_utils import run_bass_kernel_spmd

F32 = mybir.dt.float32
BF16 = mybir.dt.bfloat16
AF = mybir.ActivationFunctionType
ALU = mybir.AluOpType
AX = mybir.AxisListType

NCORES = 8
T_ALL = 4096
TOK = 256
NT = T_ALL // TOK

T_SLOTS = ['xnT', 'router', 'q0', 'q1', 'prepA', 'prepB', 'lat',
           'kv0', 'kv1', 'kv2', 'kv3', 'kv4', 'kv5', 'kv6', 'kv7',
           'vT0', 'qk0a', 'qk0', 'vT1', 'qk1a', 'qk1', 'av0', 'av1', 'expert',
           'wout', 'yT']
S_SLOTS = ['apply', 'den', 'qev0', 'qev1', 'latev', 'kv0ev', 'kv1ev', 'kv2ev', 'kv3ev',
           'kv4ev', 'kv5ev', 'kv6ev', 'kv7ev',
           'vst0', 'extr0', 'vst1', 'exp0', 'extr1', 'exp1', 'of0', 'of1', 'nrinv']
V_SLOTS = ['xnev', 'lgev', 'router', 'indwev', 'mrep', 'qkb', 'vb0',
           'vb1', 'xe0', 'xe1', 'sm0', 'sct0', 'pav0', 'sm1', 'sct1', 'pav1',
           'nstats', 'nnegb', 'yev', 'yadd']
T_PER, S_PER, V_PER = len(T_SLOTS), len(S_SLOTS), len(V_SLOTS)
P0_T = 4 + 64
P0_S = 66
P0_V = 55
P0_SEX = 16 * 8


def tv(i, slot):
    return P0_T + i * T_PER + T_SLOTS.index(slot) + 1


def sv(i, slot):
    return P0_S + i * S_PER + S_SLOTS.index(slot) + 1


def vv(i, slot):
    return P0_V + i * V_PER + V_SLOTS.index(slot) + 1


def build_core(dbg_dump=None, nt=NT, phase0_only=False):
    nc = bass.Bass("TRN2", target_bir_lowering=False)

    x_d = nc.dram_tensor("x", [T_ALL, 1024], F32, kind="ExternalInput").ap()
    wq_d = nc.dram_tensor("wq", [1024, 1024], F32, kind="ExternalInput").ap()
    bq_d = nc.dram_tensor("bq", [1024], F32, kind="ExternalInput").ap()
    wdn_d = nc.dram_tensor("wdn", [1024, 256], F32, kind="ExternalInput").ap()
    bdn_d = nc.dram_tensor("bdn", [256], F32, kind="ExternalInput").ap()
    wup_d = nc.dram_tensor("wup", [256, 2048], F32, kind="ExternalInput").ap()
    bup_d = nc.dram_tensor("bup", [2048], F32, kind="ExternalInput").ap()
    wo_d = nc.dram_tensor("wo", [1024, 1024], F32, kind="ExternalInput").ap()
    bo_d = nc.dram_tensor("bo", [1024], F32, kind="ExternalInput").ap()
    rw_d = nc.dram_tensor("rw", [8, 1024], F32, kind="ExternalInput").ap()
    rb_d = nc.dram_tensor("rb", [8], F32, kind="ExternalInput").ap()
    exw_d = nc.dram_tensor("exw", [8, 1024, 1024], F32, kind="ExternalInput").ap()
    exb_d = nc.dram_tensor("exb", [8, 1024], F32, kind="ExternalInput").ap()
    out_d = nc.dram_tensor("out", [T_ALL, 1024], F32, kind="ExternalOutput").ap()
    exs_d = nc.dram_tensor("exscratch", [8, 128, 8192], BF16).ap()
    if dbg_dump is not None:
        dbg_d = nc.dram_tensor("dbg", [dbg_dump[1], dbg_dump[2]], F32,
                               kind="ExternalOutput").ap()

    from contextlib import ExitStack
    with ExitStack() as st:
        e = st.enter_context
        wq = e(nc.sbuf_tensor("wq_sb", [128, 8192], BF16))
        wdn = e(nc.sbuf_tensor("wdn_sb", [128, 2048], BF16))
        wup = e(nc.sbuf_tensor("wup_sb", [128, 8192], BF16))
        wo = e(nc.sbuf_tensor("wo_sb", [128, 8192], BF16))
        rT = e(nc.sbuf_tensor("rT_sb", [128, 64], F32))
        rbf = e(nc.sbuf_tensor("rb_full", [128, 8], F32))
        bo_r = e(nc.sbuf_tensor("bo_row", [1, 1024], BF16))
        qpatb = e(nc.sbuf_tensor("qpatb", [128, 16], BF16))
        kpatb = e(nc.sbuf_tensor("kpatb", [128, 16], BF16))
        vpatb = e(nc.sbuf_tensor("vpatb", [64, 16], BF16))
        bdn_p = e(nc.sbuf_tensor("bdn_pp", [128, 2], F32))
        exb = e(nc.sbuf_tensor("exb_sb", [8, 1024], BF16))
        unit = e(nc.sbuf_tensor("unit_sb", [8, 1024], BF16))
        id32 = e(nc.sbuf_tensor("id32", [128, 128], F32))
        idbf = e(nc.sbuf_tensor("idbf", [128, 128], BF16))
        icol = e(nc.sbuf_tensor("icol", [128, 1], F32))
        iota16 = e(nc.sbuf_tensor("iota16", [128, 8], F32))
        epsc = e(nc.sbuf_tensor("epsc", [128, 1], F32))
        ones_b = e(nc.sbuf_tensor("ones_b", [1, 256], BF16))
        ones_f = e(nc.sbuf_tensor("ones_f", [1, 128], F32))
        x_t = e(nc.sbuf_tensor("x_t", [128, 4096], F32))
        xn_t = e(nc.sbuf_tensor("xn_t", [128, 2048], F32))
        xnf32 = e(nc.sbuf_tensor("xnf32", [128, 2048], F32))
        xnf = e(nc.sbuf_tensor("xnf", [128, 2048], BF16))
        lat = e(nc.sbuf_tensor("lat", [128, 512], BF16))
        qoct = e(nc.sbuf_tensor("qoct", [128, 4096], BF16))
        koct = e(nc.sbuf_tensor("koct", [128, 4096], BF16))
        vdm = e(nc.sbuf_tensor("vdm", [64, 4096], BF16))
        vst = e(nc.sbuf_tensor("vst", [128, 4096], BF16))
        p32 = e(nc.sbuf_tensor("p32", [128, 512], F32))
        pex32 = e(nc.sbuf_tensor("pex32", [128, 512], F32))
        maskb = e(nc.sbuf_tensor("maskb", [128, 32], F32))
        mb1 = e(nc.sbuf_tensor("mb1", [128, 1], F32))
        umat = e(nc.sbuf_tensor("umat", [2, 128], BF16))
        emat = e(nc.sbuf_tensor("emat", [2, 32], BF16))
        mx16 = e(nc.sbuf_tensor("mx16", [128, 16], F32))
        s16 = e(nc.sbuf_tensor("s16", [128, 16], F32))
        r16 = e(nc.sbuf_tensor("r16", [128, 16], F32))
        pdiag = e(nc.sbuf_tensor("pdiag", [128, 2048], BF16))
        pav = e(nc.sbuf_tensor("pav", [128, 2048], BF16))
        o_f = e(nc.sbuf_tensor("o_f", [128, 2048], BF16))
        mrep = e(nc.sbuf_tensor("mrep", [128, 2048], BF16))
        indw_t = e(nc.sbuf_tensor("indw_t", [128, 16], BF16))
        indw_f = e(nc.sbuf_tensor("indw_f", [8, 256], BF16))
        xe = e(nc.sbuf_tensor("xe", [128, 4096], BF16))
        y_f = e(nc.sbuf_tensor("y_f", [128, 2048], F32))
        exbuf = e(nc.sbuf_tensor("exbuf", [128, 16384], BF16))
        outsb = e(nc.sbuf_tensor("outsb", [128, 2048], F32))
        wstage = y_f
        irow = y_f[:, 1024:1152]
        lg = e(nc.sbuf_tensor("lg", [128, 16], F32))
        eqv = e(nc.sbuf_tensor("eqv", [128, 16], F32))
        selv = e(nc.sbuf_tensor("selv", [128, 16], F32))
        idxv = e(nc.sbuf_tensor("idxv", [128, 2], F32))
        mxl = e(nc.sbuf_tensor("mxl", [128, 2], F32))
        nmxl = e(nc.sbuf_tensor("nmxl", [128, 2], F32))
        denl = e(nc.sbuf_tensor("denl", [128, 2], F32))
        wgt = e(nc.sbuf_tensor("wgt", [128, 2], F32))
        expl = e(nc.sbuf_tensor("expl", [128, 16], F32))
        stats = e(nc.sbuf_tensor("stats", [128, 24], F32))
        mv = e(nc.sbuf_tensor("mv", [128, 4], F32))
        rinv = e(nc.sbuf_tensor("rinv", [128, 2], F32))
        lnv = e(nc.sbuf_tensor("lnv", [128, 2], F32))
        negb = e(nc.sbuf_tensor("negb", [128, 2], F32))
        psb = e(nc.psum_tensor("psb", [128, 2048], F32))
        psa = e(nc.psum_tensor("psa", [128, 2048], F32))
        psa_bf = psa[:].bitcast(BF16)

        sx0 = e(nc.semaphore("sx0"))
        sx1 = e(nc.semaphore("sx1"))
        sex0 = e(nc.semaphore("sex0"))
        sex1 = e(nc.semaphore("sex1"))
        sexg0 = e(nc.semaphore("sexg0"))
        sexg1 = e(nc.semaphore("sexg1"))
        sexp = e(nc.semaphore("sexp"))
        sw = e(nc.semaphore("sw"))
        smT = e(nc.semaphore("smT"))
        smS = e(nc.semaphore("smS"))
        smV = e(nc.semaphore("smV"))
        sxv = e(nc.semaphore("sxv"))
        sxu = e(nc.semaphore("sxu"))
        so = e(nc.semaphore("so"))
        smG = e(nc.semaphore("smG"))
        block = e(nc.Block())

        def re(ap_, pat, **kw):
            return ap_.rearrange(pat, **kw)

        @block.gpsimd
        def _(gpsimd):
            gpsimd.iota(icol[:], [[0, 1]], channel_multiplier=1,
                        allow_small_or_imprecise_dtypes=True).then_inc(smG, 1)
            gpsimd.iota(irow, [[1, 128]], channel_multiplier=0,
                        allow_small_or_imprecise_dtypes=True).then_inc(smG, 1)
            gpsimd.iota(iota16[:], [[1, 8]], channel_multiplier=0, base=16,
                        allow_small_or_imprecise_dtypes=True).then_inc(smG, 1)
            gpsimd.iota(y_f[0:2, 1152:1280], [[1, 8], [0, 16]], channel_multiplier=0,
                        allow_small_or_imprecise_dtypes=True).then_inc(smG, 1)
            gpsimd.iota(y_f[0:2, 1280:1408], [[1, 4], [0, 32]], channel_multiplier=0,
                        allow_small_or_imprecise_dtypes=True).then_inc(smG, 1)
            gpsimd.iota(y_f[0:2, 1408:1440], [[1, 2], [0, 16]], channel_multiplier=0,
                        allow_small_or_imprecise_dtypes=True).then_inc(smG, 1)
            gpsimd.nop().then_inc(smV, 1)

            def gdma(dst, src, sem):
                gpsimd.dma_start(out=dst, in_=src).then_inc(sem, 16)

            def xload(j):
                xo = (j % 2) * 2048
                sxi = sx0 if j % 2 == 0 else sx1
                gdma(x_t[:, xo:xo + 1024], x_d[TOK * j:TOK * j + 128, :], sxi)
                gdma(x_t[:, xo + 1024:xo + 2048],
                     x_d[TOK * j + 128:TOK * j + 256, :], sxi)

            if not phase0_only:
                xload(0)
                if nt > 1:
                    xload(1)
            for ex in range(8):
                gpsimd.wait_ge(smS, 1 + 8 * (ex + 1))
                if ex > 0:
                    gpsimd.wait_ge(sexp, 16 * ex)
                gdma(exs_d[ex], exbuf[:, 8192:16384], sexp)
            if not phase0_only:
                for i in range(nt):
                    for ex in range(8):
                        n = 8 * i + ex
                        if n == 0:
                            gpsimd.wait_ge(smT, P0_T)
                            gpsimd.wait_ge(sexp, P0_SEX)
                        if n >= 2:
                            gpsimd.wait_ge(sxu, n - 1)
                        gdma(exbuf[:, 8192 * (n % 2) + 4096:8192 * (n % 2) + 8192],
                             exs_d[ex][:, 4096:8192],
                             sexg0 if n % 2 == 0 else sexg1)
                    gpsimd.wait_ge(smV, vv(i, 'yadd'))
                    if i + 2 < nt:
                        xload(i + 2)
                    for g in range(2):
                        gdma(out_d[TOK * i + 128 * g:TOK * i + 128 * (g + 1), :],
                             outsb[:, 1024 * g:1024 * (g + 1)], so)

        @block.vector
        def _(vector):
            cV = [1]

            def vinc(ins=None):
                if ins is None:
                    ins = vector.drain()
                ins.then_inc(smV, 1)
                cV[0] += 1

            vector.wait_ge(smG, 6)
            vector.tensor_scalar(out=id32[:], in0=irow, scalar1=icol[:, 0:1],
                                 scalar2=None, op0=ALU.is_equal)
            vector.drain()
            vector.tensor_copy(out=idbf[:], in_=id32[:])
            vector.memset(qoct[:], 0.0)
            vector.memset(pdiag[:], 0.0)
            vector.memset(epsc[:], 1e-5)
            vector.memset(ones_b[:], 1.0)
            vector.memset(ones_f[:], 1.0)
            tmpi = irow[0:8, 0:8].unsqueeze(2).broadcast_to([8, 8, 128])
            vector.tensor_scalar(out=re(unit[:, :], "p (ee c) -> p ee c", c=128),
                                 in0=tmpi, scalar1=icol[0:8, 0:1],
                                 scalar2=None, op0=ALU.is_equal)
            vector.memset(mb1[:], -1.0e9)
            vector.scalar_tensor_tensor(out=y_f[0:2, 1440:1568], in0=y_f[0:2, 1280:1408],
                                        scalar=-2.0, in1=y_f[0:2, 1152:1280],
                                        op0=ALU.mult, op1=ALU.add)
            vector.drain()
            vector.tensor_scalar(out=umat[:], in0=y_f[0:2, 1440:1568],
                                 scalar1=icol[0:2, 0:1], scalar2=None, op0=ALU.is_equal)
            vector.tensor_scalar(out=emat[:], in0=y_f[0:2, 1408:1440],
                                 scalar1=icol[0:2, 0:1], scalar2=None, op0=ALU.is_equal)
            vector.drain()
            vinc()
            wcount = 0
            for r in range(4):
                wcount += 2
                vector.wait_ge(sw, 16 * wcount)
                vinc(vector.tensor_copy(out=wq[:, 2048 * r:2048 * (r + 1)], in_=wstage[:]))
            wcount += 8
            vector.wait_ge(sw, 16 * wcount)
            vinc(vector.tensor_copy(out=wdn[:], in_=wstage[:]))
            for kc in range(2):
                wcount += 1
                vector.wait_ge(sw, 16 * wcount)
                dk = re(wup[:, 4096 * kc:4096 * kc + 2048], "p (h half d) -> p h half d",
                        half=2, d=64)
                sk = re(wstage[:, :], "p (h x) -> p h x", x=128).unsqueeze(2)[:, :, :, 0:64]
                vector.tensor_copy(out=dk, in_=sk.broadcast_to([128, 16, 2, 64]))
                dvv = re(wup[:, 4096 * kc + 2048:4096 * kc + 4096],
                         "p (h half d) -> p h half d", half=2, d=64)
                svv = re(wstage[:, :], "p (h x) -> p h x", x=128).unsqueeze(2)[:, :, :, 64:128]
                vinc(vector.tensor_copy(out=dvv, in_=svv.broadcast_to([128, 16, 2, 64])))
            for r in range(4):
                wcount += 2
                vector.wait_ge(sw, 16 * wcount)
                vinc(vector.tensor_copy(out=wo[:, 2048 * r:2048 * (r + 1)], in_=wstage[:]))
            wcount += 1
            vector.wait_ge(sw, 16 * wcount)
            vinc()
            vector.wait_ge(smT, 1)
            vinc(vector.tensor_copy(out=rT[:], in_=psa[:, 0:64]))
            vector.memset(wstage[0:16, 0:128], 0.0)
            vinc()
            wcount += 5
            vector.wait_ge(smT, 2)
            srcq0 = re(psa[:, 80:96], "p (par m) -> p par m", m=8)
            dstq0 = re(qpatb[:], "p (m par) -> p par m", par=2)
            vector.tensor_scalar(out=dstq0, in0=srcq0, scalar1=0.125, scalar2=None,
                                 op0=ALU.mult)
            vector.tensor_copy(out=kpatb[:], in_=psa[:, 96:112])
            vector.tensor_copy(out=vpatb[:], in_=psa[0:64, 112:128])
            vinc()
            wcount += 3
            vector.wait_ge(sw, 16 * wcount)
            vinc(vector.tensor_copy(out=bo_r[:], in_=wstage[0:1, 0:1024]))
            vinc()
            vector.wait_ge(smT, 3)
            vinc(vector.tensor_copy(out=rbf[:], in_=psa[:, 64:72]))
            wcount += 1
            vector.wait_ge(sw, 16 * wcount)
            vinc(vector.tensor_copy(out=exb[:], in_=wstage[0:8, 0:1024]))
            for ex in range(8):
                for sr in range(4):
                    wcount += 2
                    vector.wait_ge(sw, 16 * wcount)
                    if sr == 0 and ex > 0:
                        vector.wait_ge(smT, 4 + 8 * ex)
                    vinc(vector.tensor_copy(out=exbuf[:, 2048 * sr:2048 * (sr + 1)],
                                            in_=wstage[:]))
            assert cV[0] == P0_V - 2, cV[0]
            if phase0_only:
                return

            def ln_front(j):
                xo2 = (j % 2) * 2048
                vector.wait_ge(sx0 if j % 2 == 0 else sx1, 32 * (j // 2 + 1))
                for g in range(2):
                    vector.bn_stats(out=stats[:, 12 * g:12 * g + 6],
                                    in_=x_t[:, xo2 + 1024 * g:xo2 + 1024 * g + 512])
                    vector.bn_stats(out=stats[:, 12 * g + 6:12 * g + 12],
                                    in_=x_t[:, xo2 + 1024 * g + 512:
                                            xo2 + 1024 * g + 1024])
                vector.drain()
                for g in range(2):
                    vector.bn_aggr(out=mv[:, 2 * g:2 * g + 2],
                                   in_=stats[:, 12 * g:12 * g + 12])
                vector.drain()

            def negb_calc():
                vector.scalar_tensor_tensor(
                    out=negb[:], in0=re(mv[:], "p (g two) -> p g two", two=2)[:, :, 0],
                    scalar=-1.0, in1=rinv[:], op0=ALU.mult, op1=ALU.mult)
                vector.drain()

            ln_front(0)
            vinc()
            vector.wait_ge(smS, P0_S)
            negb_calc()
            vinc()
            for i in range(nt):
                xo = (i % 2) * 2048
                vector.wait_ge(smT, tv(i, 'xnT'))
                if i > 0:
                    vector.wait_ge(smT, tv(i - 1, 'lat'))
                vector.tensor_copy(out=xnf32[:], in_=psb[:])
                dst2 = re(re(xnf[:], "p (cc t) -> p cc t", t=256),
                          "p cc (g x) -> p g cc x", g=2)
                src2 = re(psb[:], "p (g cc x) -> p g cc x", cc=8, x=128)
                vector.tensor_copy(out=dst2, in_=src2)
                vinc()
                vector.wait_ge(smT, tv(i, 'router'))
                vector.tensor_copy(out=lg[:, 0:8], in_=psa[:, 512:520])
                vector.tensor_copy(out=lg[:, 8:16], in_=psa[:, 1024:1032])
                vector.drain()
                src_rb = rbf[:].unsqueeze(1).broadcast_to([128, 2, 8])
                vector.tensor_tensor(out=re(lg[:], "p (g ee) -> p g ee", ee=8),
                                     in0=re(lg[:], "p (g ee) -> p g ee", ee=8),
                                     in1=src_rb, op=ALU.add)
                vector.drain()
                vector.tensor_reduce(out=mxl[:], in_=re(lg[:], "p (g ee) -> p g ee", ee=8),
                                     axis=AX.X, op=ALU.max)
                vector.drain()
                vector.tensor_scalar_mul(out=nmxl[:], in0=mxl[:], scalar1=-1.0)
                vector.drain()
                vinc()
                for g in range(2):
                    vector.tensor_scalar(out=eqv[:, 8 * g:8 * g + 8],
                                         in0=lg[:, 8 * g:8 * g + 8],
                                         scalar1=mxl[:, g:g + 1], scalar2=None,
                                         op0=ALU.is_ge)
                vector.drain()
                for g in range(2):
                    vector.scalar_tensor_tensor(out=selv[:, 8 * g:8 * g + 8],
                                                in0=eqv[:, 8 * g:8 * g + 8], scalar=-16.0,
                                                in1=iota16[:], op0=ALU.mult, op1=ALU.add)
                vector.drain()
                vector.tensor_reduce(out=idxv[:], in_=re(selv[:], "p (g ee) -> p g ee", ee=8),
                                     axis=AX.X, op=ALU.min)
                vector.drain()
                vector.wait_ge(smS, sv(i, 'den'))
                vector.reciprocal(out=wgt[:], in_=denl[:])
                vector.drain()
                for g in range(2):
                    vector.tensor_scalar(out=eqv[:, 8 * g:8 * g + 8], in0=iota16[:],
                                         scalar1=idxv[:, g:g + 1], scalar2=16.0,
                                         op0=ALU.subtract, op1=ALU.is_equal)
                vector.drain()
                for g in range(2):
                    vector.tensor_scalar(out=indw_t[:, 8 * g:8 * g + 8],
                                         in0=eqv[:, 8 * g:8 * g + 8],
                                         scalar1=wgt[:, g:g + 1], scalar2=None,
                                         op0=ALU.mult)
                vector.drain()
                vinc()
                vector.wait_ge(smT, tv(i, 'prepA'))
                vinc(vector.tensor_copy(out=indw_f[:], in_=psa_bf[0:8, 3072:3328]))
                vector.wait_ge(smT, tv(i, 'prepB'))
                vector.tensor_copy(out=mrep[:], in_=psa[:])
                vector.drain()
                vinc()
                vector.wait_ge(smS, sv(i, 'qev1'))
                vector.wait_ge(smS, sv(i, 'kv3ev'))
                q3 = re(qoct[:], "p (oj h) -> p oj h", h=16)
                vector.tensor_tensor(out=q3, in0=q3,
                                     in1=qpatb[:].unsqueeze(1).broadcast_to([128, 256, 16]),
                                     op=ALU.add)
                k3 = re(koct[:], "p (oj g) -> p oj g", g=16)
                vector.tensor_tensor(out=k3, in0=k3,
                                     in1=kpatb[:].unsqueeze(1).broadcast_to([128, 256, 16]),
                                     op=ALU.add)
                vector.drain()
                vinc()
                vector.wait_ge(smS, sv(i, 'kv7ev'))
                for w in range(2):
                    v3 = re(vdm[:, 2048 * w:2048 * (w + 1)], "p (oj g) -> p oj g", g=16)
                    vector.tensor_tensor(
                        out=v3, in0=v3,
                        in1=vpatb[:].unsqueeze(1).broadcast_to([64, 128, 16]),
                        op=ALU.add)
                    vector.drain()
                    vinc()
                for ex_i in range(2):
                    n = 8 * i + ex_i
                    if n >= 2:
                        vector.wait_ge(sxu, n - 1)
                    src0 = re(xnf[:], "p (cc t) -> p cc t", t=256)
                    m1 = mrep[:, 256 * ex_i:256 * (ex_i + 1)].unsqueeze(1)
                    vector.tensor_tensor(
                        out=re(xe[:, 2048 * (ex_i % 2):2048 * (ex_i % 2) + 2048],
                               "p (cc t) -> p cc t", t=256),
                        in0=src0, in1=m1.broadcast_to([128, 8, 256]),
                        op=ALU.mult).then_inc(sxv, 1)
                    vinc()
                for w in range(2):
                    vector.wait_ge(smS, sv(i, 'extr%d' % w))
                    mb3 = maskb[:].unsqueeze(1).broadcast_to([128, 16, 32])
                    vector.tensor_tensor(out=re(p32[:], "p (o c) -> p o c", c=32),
                                         in0=re(p32[:], "p (o c) -> p o c", c=32),
                                         in1=mb3, op=ALU.add)
                    vector.drain()
                    vector.tensor_reduce(out=mx16[:],
                                         in_=re(p32[:], "p (o c) -> p o c", c=32),
                                         axis=AX.X, op=ALU.max)
                    vector.drain()
                    m3 = mx16[:].unsqueeze(2).broadcast_to([128, 16, 32])
                    vector.tensor_tensor(out=re(p32[:], "p (o c) -> p o c", c=32),
                                         in0=re(p32[:], "p (o c) -> p o c", c=32),
                                         in1=m3, op=ALU.subtract)
                    vector.drain()
                    vinc()
                    vector.wait_ge(smS, sv(i, 'exp%d' % w))
                    vector.tensor_reduce(out=s16[:],
                                         in_=re(pex32[:], "p (o c) -> p o c", c=32),
                                         axis=AX.X, op=ALU.add)
                    vector.drain()
                    vector.reciprocal(out=r16[:], in_=s16[:])
                    vector.drain()
                    for a in range(4):
                        srcn = re(pex32[32 * a:32 * a + 32, :], "p (o c) -> p o c", c=32)
                        r3 = r16[32 * a:32 * a + 32, :].unsqueeze(2)
                        dstn = re(pdiag[32 * a:32 * a + 32, :], "p (o x) -> p o x",
                                  x=128)[:, :, 32 * a:32 * a + 32]
                        vector.tensor_tensor(out=dstn, in0=srcn,
                                             in1=r3.broadcast_to([32, 16, 32]),
                                             op=ALU.mult)
                    vector.drain()
                    vinc()
                    if w == 1:
                        vector.wait_ge(smT, tv(i, 'av0'))
                    elif i > 0:
                        vector.wait_ge(smT, tv(i - 1, 'av1'))
                    vector.transpose(out=pav[:], in_=pdiag[:])
                    vector.drain()
                    vinc()
                for ex_i in range(2, 8):
                    n = 8 * i + ex_i
                    if n >= 2:
                        vector.wait_ge(sxu, n - 1)
                    src0 = re(xnf[:], "p (cc t) -> p cc t", t=256)
                    m1 = mrep[:, 256 * ex_i:256 * (ex_i + 1)].unsqueeze(1)
                    vector.tensor_tensor(
                        out=re(xe[:, 2048 * (ex_i % 2):2048 * (ex_i % 2) + 2048],
                               "p (cc t) -> p cc t", t=256),
                        in0=src0, in1=m1.broadcast_to([128, 8, 256]),
                        op=ALU.mult).then_inc(sxv, 1)
                if i + 1 < nt:
                    ln_front(i + 1)
                vinc()
                if i + 1 < nt:
                    vector.wait_ge(smS, sv(i, 'nrinv'))
                    vector.wait_ge(smS, sv(i, 'apply'))
                    negb_calc()
                vinc()
                vector.wait_ge(smT, tv(i, 'wout'))
                if i > 0:
                    vector.wait_ge(smT, tv(i - 1, 'yT'))
                vector.tensor_copy(out=y_f[:], in_=psb[:])
                vector.drain()
                vinc()
                vector.wait_ge(smT, tv(i, 'yT'))
                if i > 0:
                    vector.wait_ge(so, 32 * i)
                for g in range(2):
                    vector.tensor_tensor(out=outsb[:, 1024 * g:1024 * (g + 1)],
                                         in0=psa[:, 1024 * g:1024 * (g + 1)],
                                         in1=x_t[:, xo + 1024 * g:xo + 1024 * (g + 1)],
                                         op=ALU.add)
                vector.drain()
                vinc()
            assert cV[0] == P0_V + nt * V_PER, cV[0]

        @block.scalar
        def _(scalar):
            cS = [0]

            def sinc(ins=None):
                if ins is None:
                    ins = scalar.drain()
                ins.then_inc(smS, 1)
                cS[0] += 1

            scalar.wait_ge(smT, 4)
            sinc(scalar.activation(out=maskb[:], in_=psa[:, 128:160], func=AF.Identity,
                                   scale=1.0e9, bias=mb1[:, 0:1]))
            for ex in range(8):
                for kd in range(8):
                    scalar.wait_ge(smT, 4 + 8 * ex + kd + 1)
                    if kd == 0 and ex > 0:
                        scalar.wait_ge(sexp, 16 * ex)
                    sinc(scalar.copy(
                        out=exbuf[:, 8192 + 1024 * kd:8192 + 1024 * (kd + 1)],
                        in_=psa_bf[:, 1024 * (kd % 4):1024 * (kd % 4) + 1024]))
            assert cS[0] == P0_S - 1, cS[0]
            if phase0_only:
                return

            def rinv_calc():
                scalar.activation(out=lnv[:],
                                  in_=re(mv[:], "p (g two) -> p g two", two=2)[:, :, 1],
                                  func=AF.Ln, bias=epsc[:, 0:1])
                scalar.drain()
                scalar.activation(out=rinv[:], in_=lnv[:], func=AF.Exp, scale=-0.5)
                scalar.drain()

            scalar.wait_ge(smV, P0_V - 1)
            rinv_calc()
            sinc()
            for i in range(nt):
                xo = (i % 2) * 2048
                scalar.wait_ge(smV, vv(i - 1, 'nnegb') if i > 0 else P0_V)
                if i > 0:
                    scalar.wait_ge(smT, tv(i - 1, 'xnT'))
                for g in range(2):
                    scalar.activation(out=xn_t[:, 1024 * g:1024 * (g + 1)],
                                      in_=x_t[:, xo + 1024 * g:xo + 1024 * (g + 1)],
                                      func=AF.Identity, scale=rinv[:, g:g + 1],
                                      bias=negb[:, g:g + 1])
                sinc()
                scalar.wait_ge(smV, vv(i, 'lgev'))
                for g in range(2):
                    scalar.activation(out=expl[:, 8 * g:8 * g + 8],
                                      in_=lg[:, 8 * g:8 * g + 8],
                                      func=AF.Exp, bias=nmxl[:, g:g + 1],
                                      accum_out=denl[:, g:g + 1])
                sinc()
                for mh in range(2):
                    scalar.wait_ge(smT, tv(i, 'q%d' % mh))
                    if mh == 0 and i > 0:
                        scalar.wait_ge(smT, tv(i - 1, 'qk1'))
                    for hl in range(2):
                        srcq = re(psb[64 * hl:64 * hl + 64, 1024 * mh:1024 * (mh + 1)],
                                  "p (cc og j) -> p og j cc", og=32, j=8)
                        dstq = re(qoct[64 * hl:64 * hl + 64, :],
                                  "p (og j cc two) -> p og j cc two", j=8, cc=8,
                                  two=2)[:, :, :, 4 * mh:4 * mh + 4, hl]
                        scalar.activation(out=dstq, in_=srcq, func=AF.Identity,
                                          scale=0.125)
                    sinc()
                scalar.wait_ge(smT, tv(i, 'lat'))
                for kc in range(2):
                    scalar.activation(out=lat[:, 256 * kc:256 * (kc + 1)],
                                      in_=psa[:, 256 * kc:256 * (kc + 1)],
                                      func=AF.Identity, bias=bdn_p[:, kc:kc + 1])
                sinc()
                for r in range(8):
                    scalar.wait_ge(smT, tv(i, 'kv%d' % r))
                    if r == 0 and i > 0:
                        scalar.wait_ge(smT, tv(i - 1, 'qk1'))
                    if r == 4 and i > 0:
                        scalar.wait_ge(smT, tv(i - 1, 'vT1'))
                    hb = 1024 * (r % 2)
                    if r < 4:
                        srck = re(psb[:, hb:hb + 1024], "p (g og j) -> p og j g",
                                  og=32, j=8)
                        dstk = re(koct[:], "p (og j gg) -> p og j gg", j=8,
                                  gg=16)[:, :, :, 4 * r:4 * r + 4]
                        scalar.copy(out=dstk, in_=srck)
                    else:
                        srcv = re(psb[0:64, hb:hb + 1024], "p (g og j) -> p og j g",
                                  og=32, j=8)
                        dstv = re(vdm[:], "p (og j gg) -> p og j gg", j=8,
                                  gg=16)[:, :, :, 4 * (r - 4):4 * (r - 4) + 4]
                        scalar.copy(out=dstv, in_=srcv)
                    sinc()
                scalar.wait_ge(smT, tv(i, 'vT0'))
                if i > 0:
                    scalar.wait_ge(smT, tv(i - 1, 'av0'))
                srcv = re(psa_bf[:, 0:1024], "p (o d) -> p o d", d=64).unsqueeze(2)
                dstv = re(vst[:, 0:2048], "p (o hl d) -> p o hl d", hl=2, d=64)
                scalar.copy(out=dstv, in_=srcv.broadcast_to([128, 16, 2, 64]))
                sinc()
                scalar.wait_ge(smT, tv(i, 'qk0'))
                for a in range(4):
                    srca = re(psa[32 * a:32 * a + 32, :], "p (o x) -> p o x",
                              x=128)[:, :, 32 * a:32 * a + 32]
                    dsta = re(p32[32 * a:32 * a + 32, :], "p (o c) -> p o c", c=32)
                    scalar.copy(out=dsta, in_=srca)
                sinc()
                scalar.wait_ge(smT, tv(i, 'vT1'))
                if i > 0:
                    scalar.wait_ge(smT, tv(i - 1, 'av1'))
                srcv = re(psa_bf[:, 0:1024], "p (o d) -> p o d", d=64).unsqueeze(2)
                dstv = re(vst[:, 2048:4096], "p (o hl d) -> p o hl d", hl=2, d=64)
                scalar.copy(out=dstv, in_=srcv.broadcast_to([128, 16, 2, 64]))
                sinc()
                scalar.wait_ge(smV, vv(i, 'sm0'))
                scalar.activation(out=pex32[:], in_=p32[:], func=AF.Exp)
                sinc(scalar.drain())
                scalar.wait_ge(smT, tv(i, 'qk1'))
                for a in range(4):
                    srca = re(psa[32 * a:32 * a + 32, :], "p (o x) -> p o x",
                              x=128)[:, :, 32 * a:32 * a + 32]
                    dsta = re(p32[32 * a:32 * a + 32, :], "p (o c) -> p o c", c=32)
                    scalar.copy(out=dsta, in_=srca)
                sinc()
                scalar.wait_ge(smV, vv(i, 'sm1'))
                scalar.wait_ge(smV, vv(i, 'sct0'))
                scalar.activation(out=pex32[:], in_=p32[:], func=AF.Exp)
                sinc(scalar.drain())
                scalar.wait_ge(smT, tv(i, 'av0'))
                if i > 0:
                    scalar.wait_ge(smT, tv(i - 1, 'wout'))
                for hl in range(2):
                    for a in range(4):
                        srco = re(psa[64 * hl:64 * hl + 64, :],
                                  "p (aa ol jl cc two) -> p two aa cc ol jl",
                                  ol=16, jl=2, cc=8, two=2)[:, hl, a]
                        dsto = re(o_f[64 * hl:64 * hl + 64, :],
                                  "p (cc ww ol aa jl) -> p ww aa cc ol jl",
                                  ww=2, ol=16, aa=4, jl=2)[:, 0, a]
                        scalar.copy(out=dsto, in_=srco)
                sinc()
                scalar.wait_ge(smT, tv(i, 'av1'))
                for hl in range(2):
                    for a in range(4):
                        srco = re(psa[64 * hl:64 * hl + 64, :],
                                  "p (aa ol jl cc two) -> p two aa cc ol jl",
                                  ol=16, jl=2, cc=8, two=2)[:, hl, a]
                        dsto = re(o_f[64 * hl:64 * hl + 64, :],
                                  "p (cc ww ol aa jl) -> p ww aa cc ol jl",
                                  ww=2, ol=16, aa=4, jl=2)[:, 1, a]
                        scalar.copy(out=dsto, in_=srco)
                sinc()
                if i + 1 < nt:
                    scalar.wait_ge(smV, vv(i, 'nstats'))
                    rinv_calc()
                sinc()
            assert cS[0] == P0_S + nt * S_PER, cS[0]

        @block.tensor
        def _(tensor):
            cT = [0]

            def tinc(ins):
                ins.then_inc(smT, 1)
                cT[0] += 1

            tensor.wait_ge(smV, 14)
            li = None
            for cc in range(8):
                li = tensor.transpose(out=psa[:, 8 * cc:8 * cc + 8],
                                      in_=wstage[0:8, 128 * cc:128 * (cc + 1)],
                                      identity=id32[0:8, 0:8])
            tinc(li)
            tensor.wait_ge(smV, 16)
            tensor.wait_ge(sw, 16 * 32)
            tensor.transpose(out=psa[:, 80:96], in_=wstage[0:16, 0:128],
                             identity=id32[0:16, 0:16])
            tensor.transpose(out=psa[:, 96:112], in_=wstage[0:16, 128:256],
                             identity=id32[0:16, 0:16])
            li = tensor.transpose(out=psa[0:64, 112:128], in_=wstage[0:16, 256:320],
                                  identity=id32[0:16, 0:16])
            tinc(li)
            tensor.wait_ge(smV, 19)
            li = tensor.matmul(out=psa[:, 64:72], lhsT=ones_f[:],
                               rhs=wstage[0:1, 1024:1032], start=True, stop=True)
            tinc(li)
            tensor.wait_ge(smV, 20)
            li = tensor.matmul(out=psa[:, 128:160], lhsT=umat[:], rhs=emat[:],
                               start=True, stop=True)
            tinc(li)
            tensor.wait_ge(smS, 1)
            for ex in range(8):
                tensor.wait_ge(smV, 21 + 4 * (ex + 1))
                for kd in range(8):
                    if 8 * ex + kd >= 2:
                        tensor.wait_ge(smS, 8 * ex + kd - 1)
                    for jo in range(8):
                        li = tensor.transpose(
                            out=psa_bf[:, 1024 * (kd % 4) + 128 * jo:
                                       1024 * (kd % 4) + 128 * (jo + 1)],
                            in_=exbuf[:, 1024 * jo + 128 * kd:1024 * jo + 128 * (kd + 1)],
                            identity=idbf[:])
                    tinc(li)
            assert cT[0] == P0_T, cT[0]
            if phase0_only:
                return
            for i in range(nt):
                tensor.wait_ge(smS, sv(i, 'apply'))
                if i > 0:
                    tensor.wait_ge(smV, vv(i - 1, 'yev'))
                for g in range(2):
                    for cc in range(8):
                        li = tensor.transpose(
                            out=psb[:, 128 * (8 * g + cc):128 * (8 * g + cc + 1)],
                            in_=xn_t[:, 1024 * g + 128 * cc:1024 * g + 128 * (cc + 1)],
                            identity=id32[:])
                tinc(li)
                tensor.wait_ge(smV, vv(i, 'xnev'))
                if i > 0:
                    tensor.wait_ge(smV, vv(i - 1, 'yadd'))
                for g in range(2):
                    off = 512 * (g + 1)
                    for cc in range(8):
                        li = tensor.matmul(out=psa[:, off:off + 8],
                                           lhsT=xnf32[:, 128 * (8 * g + cc):
                                                      128 * (8 * g + cc + 1)],
                                           rhs=rT[:, 8 * cc:8 * cc + 8],
                                           start=(cc == 0), stop=(cc == 7))
                tinc(li)
                for mh in range(2):
                    for m in range(4 * mh, 4 * mh + 4):
                        for k in range(8):
                            li = tensor.matmul(out=psb[:, 256 * m:256 * (m + 1)],
                                               lhsT=wq[:, 1024 * k + 128 * m:
                                                       1024 * k + 128 * (m + 1)],
                                               rhs=xnf[:, 256 * k:256 * (k + 1)],
                                               start=(k == 0), stop=(k == 7))
                    tinc(li)
                tensor.wait_ge(smV, vv(i, 'router'))
                for g in range(2):
                    li = tensor.transpose(out=psa_bf[0:8, 3072 + 128 * g:
                                                     3072 + 128 * (g + 1)],
                                          in_=indw_t[:, 8 * g:8 * g + 8],
                                          identity=idbf[:])
                tinc(li)
                tensor.wait_ge(smV, vv(i, 'indwev'))
                tensor.wait_ge(smV, vv(i, 'lgev'))
                for ex in range(8):
                    li = tensor.matmul(out=psa[:, 256 * ex:256 * (ex + 1)],
                                       lhsT=unit[:, 128 * ex:128 * (ex + 1)],
                                       rhs=indw_f[:], start=True, stop=True)
                tinc(li)
                tensor.wait_ge(smV, vv(i, 'mrep'))
                for m in range(2):
                    for k in range(8):
                        li = tensor.matmul(out=psa[:, 256 * m:256 * (m + 1)],
                                           lhsT=wdn[:, 256 * k + 128 * m:
                                                    256 * k + 128 * (m + 1)],
                                           rhs=xnf[:, 256 * k:256 * (k + 1)],
                                           start=(k == 0), stop=(k == 7))
                tinc(li)
                for r in range(8):
                    if r == 0:
                        tensor.wait_ge(smS, sv(i, 'latev'))
                        tensor.wait_ge(smS, sv(i, 'qev0'))
                    elif r == 1:
                        tensor.wait_ge(smS, sv(i, 'qev1'))
                    elif r >= 2:
                        tensor.wait_ge(smS, sv(i, 'kv%dev' % (r - 2)))
                    hb = 1024 * (r % 2)
                    for gch in range(4):
                        ch = 4 * r + gch
                        for k in range(2):
                            li = tensor.matmul(out=psb[:, hb + 256 * gch:
                                                       hb + 256 * (gch + 1)],
                                               lhsT=wup[:, 4096 * k + 128 * ch:
                                                        4096 * k + 128 * (ch + 1)],
                                               rhs=lat[:, 256 * k:256 * (k + 1)],
                                               start=(k == 0), stop=(k == 1))
                    tinc(li)
                tensor.wait_ge(smV, vv(i, 'vb0'))
                tensor.wait_ge(smS, sv(i, 'latev'))
                tensor.wait_ge(smV, vv(i, 'mrep'))
                for ol in range(16):
                    li = tensor.transpose(out=psa_bf[:, 64 * ol:64 * (ol + 1)],
                                          in_=vdm[:, 128 * ol:128 * (ol + 1)],
                                          identity=idbf[0:64, 0:64])
                tinc(li)
                tensor.wait_ge(smV, vv(i, 'qkb'))
                for ol in range(4, 16):
                    li = tensor.matmul(out=psa[:, 128 * ol:128 * (ol + 1)],
                                       lhsT=qoct[:, 128 * ol:128 * (ol + 1)],
                                       rhs=koct[:, 128 * ol:128 * (ol + 1)],
                                       start=True, stop=True)
                tinc(li)
                tensor.wait_ge(smS, sv(i, 'vst0'))
                for ol in range(4):
                    li = tensor.matmul(out=psa[:, 128 * ol:128 * (ol + 1)],
                                       lhsT=qoct[:, 128 * ol:128 * (ol + 1)],
                                       rhs=koct[:, 128 * ol:128 * (ol + 1)],
                                       start=True, stop=True)
                tinc(li)
                tensor.wait_ge(smS, sv(i, 'extr0'))
                tensor.wait_ge(smV, vv(i, 'vb1'))
                for ol in range(16):
                    og = 16 + ol
                    li = tensor.transpose(out=psa_bf[:, 64 * ol:64 * (ol + 1)],
                                          in_=vdm[:, 128 * og:128 * (og + 1)],
                                          identity=idbf[0:64, 0:64])
                tinc(li)
                for ol in range(4, 16):
                    og = 16 + ol
                    li = tensor.matmul(out=psa[:, 128 * ol:128 * (ol + 1)],
                                       lhsT=qoct[:, 128 * og:128 * (og + 1)],
                                       rhs=koct[:, 128 * og:128 * (og + 1)],
                                       start=True, stop=True)
                tinc(li)
                tensor.wait_ge(smS, sv(i, 'vst1'))
                for ol in range(4):
                    og = 16 + ol
                    li = tensor.matmul(out=psa[:, 128 * ol:128 * (ol + 1)],
                                       lhsT=qoct[:, 128 * og:128 * (og + 1)],
                                       rhs=koct[:, 128 * og:128 * (og + 1)],
                                       start=True, stop=True)
                tinc(li)
                def pout(m):
                    return psb[:, 256 * m:256 * (m + 1)]

                def expert_mms(ex):
                    n = 8 * i + ex
                    tensor.wait_ge(sxv, n + 1)
                    tensor.wait_ge(sex0 if n % 2 == 0 else sex1, 16 * (n // 2 + 1))
                    tensor.wait_ge(sexg0 if n % 2 == 0 else sexg1, 16 * (n // 2 + 1))
                    eb = 8192 * (n % 2)
                    xb = 2048 * (ex % 2)
                    for m in range(8):
                        for k in range(8):
                            li = tensor.matmul(out=pout(m),
                                               lhsT=exbuf[:, eb + 1024 * k + 128 * m:
                                                           eb + 1024 * k + 128 * (m + 1)],
                                               rhs=xe[:, xb + 256 * k:xb + 256 * (k + 1)],
                                               start=False, stop=False)
                    li.then_inc(sxu, 1)
                    return li

                tensor.wait_ge(smS, sv(i, 'kv7ev'))
                for b in range(4):
                    tensor.matmul(out=psb[:, 512 * b:512 * (b + 1)],
                                  lhsT=unit[0:1, 128:256], rhs=unit[0:1, 0:512],
                                  start=True, stop=False)
                expert_mms(0)
                expert_mms(1)
                tensor.wait_ge(smV, vv(i, 'pav0'))
                tensor.wait_ge(smS, sv(i, 'extr1'))
                for ol in range(16):
                    for a in range(4):
                        li = tensor.matmul(
                            out=psa[:, 512 * a + 32 * ol:512 * a + 32 * (ol + 1)],
                            lhsT=vst[32 * a:32 * a + 32, 128 * ol:128 * (ol + 1)],
                            rhs=pav[32 * a:32 * a + 32,
                                    128 * ol + 32 * a:128 * ol + 32 * a + 32],
                            start=True, stop=True,
                            tile_position=(32 * a, 0))
                tinc(li)
                expert_mms(2)
                expert_mms(3)
                tensor.wait_ge(smV, vv(i, 'pav1'))
                tensor.wait_ge(smS, sv(i, 'of0'))
                for ol in range(16):
                    for a in range(4):
                        li = tensor.matmul(
                            out=psa[:, 512 * a + 32 * ol:512 * a + 32 * (ol + 1)],
                            lhsT=vst[32 * a:32 * a + 32,
                                     2048 + 128 * ol:2048 + 128 * (ol + 1)],
                            rhs=pav[32 * a:32 * a + 32,
                                    128 * ol + 32 * a:128 * ol + 32 * a + 32],
                            start=True, stop=True,
                            tile_position=(32 * a, 0))
                tinc(li)
                for ex in range(4, 8):
                    expert_mms(ex)
                tensor.wait_ge(smS, sv(i, 'of1'))
                for m in range(8):
                    li = tensor.matmul(out=pout(m),
                                       lhsT=exb[:, 128 * m:128 * (m + 1)],
                                       rhs=indw_f[:], start=False, stop=False)
                    if m == 0:
                        tinc(li)
                    for k in range(8):
                        tensor.matmul(out=pout(m),
                                      lhsT=wo[:, 1024 * k + 128 * m:
                                              1024 * k + 128 * (m + 1)],
                                      rhs=o_f[:, 256 * k:256 * (k + 1)],
                                      start=False, stop=False)
                    li = tensor.matmul(out=pout(m),
                                       lhsT=bo_r[:, 128 * m:128 * (m + 1)],
                                       rhs=ones_b[:], start=False,
                                       stop=(m % 2 == 1))
                tinc(li)
                tensor.wait_ge(smV, vv(i, 'yev'))
                for g in range(2):
                    for cc in range(8):
                        li = tensor.transpose(out=psa[:, 1024 * g + 128 * cc:
                                                      1024 * g + 128 * (cc + 1)],
                                              in_=y_f[:, 256 * cc + 128 * g:
                                                      256 * cc + 128 * (g + 1)],
                                              identity=id32[:])
                tinc(li)
            assert cT[0] == P0_T + nt * T_PER, cT[0]

        @block.sync
        def _(sync):
            w = [0]

            def dma(dst, src, sem=None, slow=False):
                s = sem if sem is not None else sw
                sync.dma_start(out=dst, in_=src,
                               allow_slow_non_contiguous=slow).then_inc(s, 16)
                if s is sw:
                    w[0] += 1

            vc = 2
            sync.wait_ge(smV, 2)
            for r in range(4):
                if r > 0:
                    sync.wait_ge(smV, vc + r)
                dma(wstage[:, 0:1024], wq_d[128 * 2 * r:128 * (2 * r + 1), :])
                dma(wstage[:, 1024:2048], wq_d[128 * (2 * r + 1):128 * (2 * r + 2), :])
            vc += 4
            sync.wait_ge(smV, vc)
            for k in range(8):
                dma(wstage[:, 256 * k:256 * (k + 1)], wdn_d[128 * k:128 * (k + 1), :])
            vc += 1
            for kc in range(2):
                sync.wait_ge(smV, vc + kc)
                dma(wstage[:], wup_d[128 * kc:128 * (kc + 1), :])
            vc += 2
            for r in range(4):
                sync.wait_ge(smV, vc + r)
                dma(wstage[:, 0:1024], wo_d[128 * 2 * r:128 * (2 * r + 1), :])
                dma(wstage[:, 1024:2048], wo_d[128 * (2 * r + 1):128 * (2 * r + 2), :])
            vc += 4
            sync.wait_ge(smV, vc)
            dma(wstage[0:8, 0:1024], rw_d[:, :])
            vc += 1
            sync.wait_ge(smV, vc + 2)
            dma(wstage[0:8, 0:64],
                re(bq_d, "(m x) -> m x", m=8)[:, 0:64])
            dma(wstage[8:16, 64:128],
                re(bq_d, "(m x) -> m x", m=8)[:, 64:128])
            dma(wstage[0:16, 128:192],
                re(bup_d, "(h x) -> h x", h=16)[:, 0:64])
            dma(wstage[0:16, 192:256],
                re(bup_d, "(h x) -> h x", h=16)[:, 0:64])
            dma(wstage[0:16, 256:320],
                re(bup_d, "(h x) -> h x", h=16)[:, 64:128])
            sync.wait_ge(smT, 2)
            dma(wstage[0:1, 0:1024], re(bo_d, "(a b) -> a b", a=1))
            dma(bdn_p[:], re(bdn_d, "(c p) -> p c", p=128), slow=True)
            dma(wstage[0:1, 1024:1032], re(rb_d, "(a b) -> a b", a=1))
            vc += 6
            sync.wait_ge(smV, vc)
            dma(wstage[0:8, 0:1024], exb_d[:, :])
            vc += 1
            for ex in range(8):
                for sr in range(4):
                    sync.wait_ge(smV, vc + 4 * ex + sr)
                    dma(wstage[:, 0:1024], exw_d[ex, 128 * 2 * sr:128 * (2 * sr + 1), :])
                    dma(wstage[:, 1024:2048],
                        exw_d[ex, 128 * (2 * sr + 1):128 * (2 * sr + 2), :])
            if phase0_only:
                sync.wait_ge(smT, P0_T)
                sync.wait_ge(smS, P0_S - 1)
                sync.wait_ge(smV, P0_V - 2)
                sync.wait_ge(sexp, P0_SEX)
                sync.dma_start(out=out_d[0:128, :],
                               in_=maskb[:].bitcast(F32) if False else outsb[:, 0:1024]
                               ).then_inc(so, 16)
                sync.wait_ge(so, 16)
                return
            for i in range(nt):
                for ex in range(8):
                    n = 8 * i + ex
                    if n == 0:
                        sync.wait_ge(smT, P0_T)
                        sync.wait_ge(sexp, P0_SEX)
                    if n >= 2:
                        sync.wait_ge(sxu, n - 1)
                    dma(exbuf[:, 8192 * (n % 2):8192 * (n % 2) + 4096],
                        exs_d[ex][:, 0:4096],
                        sem=sex0 if n % 2 == 0 else sex1)
            sync.wait_ge(so, 32 * nt)
            if dbg_dump is not None:
                name, rr, ccn = dbg_dump
                src = {'xnf': xnf, 'xnf32': xnf32, 'xn_t': xn_t, 'qoct': qoct,
                       'koct': koct, 'vdm': vdm, 'vst': vst, 'pcomp': pcomp,
                       'pdiag': pdiag, 'pav': pav, 'o_f': o_f, 'mrep': mrep,
                       'lg': lg, 'indw_t': indw_t, 'y_f': y_f, 'lat': lat,
                       'x_t': x_t, 'outsb': outsb, 'indw_f': indw_f, 'xe': xe,
                       'pex': pex, 'mx16': mx16, 'r16': r16, 'wgt': wgt,
                       'idxv': idxv, 'wup_sb': wup, 'exbuf': exbuf}[name]
                sync.dma_start(out=dbg_d, in_=src[0:rr, 0:ccn]).then_inc(so, 16)
                sync.wait_ge(so, 32 * nt + 16)

    return nc


def kernel(L0_attn_x, L0_attn_Wq, L0_attn_bq, L0_attn_Wdown, L0_attn_bdown,
           L0_attn_Wup, L0_attn_bup, L0_attn_Wout, L0_attn_bout,
           L0_ffn_router_w, L0_ffn_router_b, L0_ffn_expert_w, L0_ffn_expert_b,
           _dbg=None):
    nc = build_core(dbg_dump=_dbg)
    in_maps = []
    for c in range(NCORES):
        in_maps.append({
            "x": np.ascontiguousarray(np.asarray(L0_attn_x)[c]),
            "wq": np.asarray(L0_attn_Wq), "bq": np.asarray(L0_attn_bq),
            "wdn": np.asarray(L0_attn_Wdown), "bdn": np.asarray(L0_attn_bdown),
            "wup": np.asarray(L0_attn_Wup), "bup": np.asarray(L0_attn_bup),
            "wo": np.asarray(L0_attn_Wout), "bo": np.asarray(L0_attn_bout),
            "rw": np.asarray(L0_ffn_router_w), "rb": np.asarray(L0_ffn_router_b),
            "exw": np.asarray(L0_ffn_expert_w), "exb": np.asarray(L0_ffn_expert_b),
        })
    res = run_bass_kernel_spmd(nc, in_maps, core_ids=list(range(NCORES)))
    out = np.stack([res.results[c]["out"] for c in range(NCORES)], axis=0)
    if _dbg is not None:
        return out, [res.results[c].get("dbg") for c in range(NCORES)]
    return out


# version 14 (final)
# speedup vs baseline: 1.1613x; 1.1613x over previous
import numpy as np
import concourse.bass as bass
import concourse.mybir as mybir
from concourse.bass_utils import run_bass_kernel_spmd

F32 = mybir.dt.float32
BF16 = mybir.dt.bfloat16
AF = mybir.ActivationFunctionType
ALU = mybir.AluOpType
AX = mybir.AxisListType

NCORES = 8
T_ALL = 4096
TOK = 256
NT = T_ALL // TOK

T_SLOTS = ['xnT', 'router', 'q0', 'q1', 'prepA', 'prepB', 'lat',
           'kv0', 'kv1', 'kv2', 'kv3', 'kv4', 'kv5', 'kv6', 'kv7',
           'vT0', 'qk0a', 'qk0', 'vT1', 'qk1a', 'qk1', 'av0', 'av1', 'expert',
           'wout', 'yT']
S_SLOTS = ['apply', 'den', 'qev0', 'qev1', 'latev', 'kv0ev', 'kv1ev', 'kv2ev', 'kv3ev',
           'kv4ev', 'kv5ev', 'kv6ev', 'kv7ev',
           'vst0', 'extr0a', 'extr0', 'vst1', 'exp0', 'extr1', 'exp1', 'of0', 'of1',
           'nrinv']
V_SLOTS = ['xnev', 'lgev', 'router', 'indwev', 'mrep', 'qkb', 'vb0',
           'vb1', 'xe0', 'xe1', 'sm0', 'sct0', 'pav0', 'sm1', 'sct1', 'pav1',
           'nstats', 'nnegb', 'yev', 'yadd']
T_PER, S_PER, V_PER = len(T_SLOTS), len(S_SLOTS), len(V_SLOTS)
P0_T = 4 + 64
P0_S = 66
P0_V = 55
P0_SEX = 16 * 8


def tv(i, slot):
    return P0_T + i * T_PER + T_SLOTS.index(slot) + 1


def sv(i, slot):
    return P0_S + i * S_PER + S_SLOTS.index(slot) + 1


def vv(i, slot):
    return P0_V + i * V_PER + V_SLOTS.index(slot) + 1


def build_core(dbg_dump=None, nt=NT, phase0_only=False):
    nc = bass.Bass("TRN2", target_bir_lowering=False)

    x_d = nc.dram_tensor("x", [T_ALL, 1024], F32, kind="ExternalInput").ap()
    wq_d = nc.dram_tensor("wq", [1024, 1024], F32, kind="ExternalInput").ap()
    bq_d = nc.dram_tensor("bq", [1024], F32, kind="ExternalInput").ap()
    wdn_d = nc.dram_tensor("wdn", [1024, 256], F32, kind="ExternalInput").ap()
    bdn_d = nc.dram_tensor("bdn", [256], F32, kind="ExternalInput").ap()
    wup_d = nc.dram_tensor("wup", [256, 2048], F32, kind="ExternalInput").ap()
    bup_d = nc.dram_tensor("bup", [2048], F32, kind="ExternalInput").ap()
    wo_d = nc.dram_tensor("wo", [1024, 1024], F32, kind="ExternalInput").ap()
    bo_d = nc.dram_tensor("bo", [1024], F32, kind="ExternalInput").ap()
    rw_d = nc.dram_tensor("rw", [8, 1024], F32, kind="ExternalInput").ap()
    rb_d = nc.dram_tensor("rb", [8], F32, kind="ExternalInput").ap()
    exw_d = nc.dram_tensor("exw", [8, 1024, 1024], F32, kind="ExternalInput").ap()
    exb_d = nc.dram_tensor("exb", [8, 1024], F32, kind="ExternalInput").ap()
    out_d = nc.dram_tensor("out", [T_ALL, 1024], F32, kind="ExternalOutput").ap()
    exs_d = nc.dram_tensor("exscratch", [8, 128, 8192], BF16).ap()
    if dbg_dump is not None:
        dbg_d = nc.dram_tensor("dbg", [dbg_dump[1], dbg_dump[2]], F32,
                               kind="ExternalOutput").ap()

    from contextlib import ExitStack
    with ExitStack() as st:
        e = st.enter_context
        wq = e(nc.sbuf_tensor("wq_sb", [128, 8192], BF16))
        wdn = e(nc.sbuf_tensor("wdn_sb", [128, 2048], BF16))
        wup = e(nc.sbuf_tensor("wup_sb", [128, 8192], BF16))
        wo = e(nc.sbuf_tensor("wo_sb", [128, 8192], BF16))
        rT = e(nc.sbuf_tensor("rT_sb", [128, 64], F32))
        rbf = e(nc.sbuf_tensor("rb_full", [128, 8], F32))
        bo_r = e(nc.sbuf_tensor("bo_row", [1, 1024], BF16))
        qpatb = e(nc.sbuf_tensor("qpatb", [128, 16], BF16))
        kpatb = e(nc.sbuf_tensor("kpatb", [128, 16], BF16))
        vpatb = e(nc.sbuf_tensor("vpatb", [64, 16], BF16))
        bdn_p = e(nc.sbuf_tensor("bdn_pp", [128, 2], F32))
        exb = e(nc.sbuf_tensor("exb_sb", [8, 1024], BF16))
        unit = e(nc.sbuf_tensor("unit_sb", [8, 1024], BF16))
        id32 = e(nc.sbuf_tensor("id32", [128, 128], F32))
        idbf = e(nc.sbuf_tensor("idbf", [128, 128], BF16))
        icol = e(nc.sbuf_tensor("icol", [128, 1], F32))
        iota16 = e(nc.sbuf_tensor("iota16", [128, 8], F32))
        epsc = e(nc.sbuf_tensor("epsc", [128, 1], F32))
        ones_b = e(nc.sbuf_tensor("ones_b", [1, 256], BF16))
        ones_f = e(nc.sbuf_tensor("ones_f", [1, 128], F32))
        x_t = e(nc.sbuf_tensor("x_t", [128, 4096], F32))
        xn_t = e(nc.sbuf_tensor("xn_t", [128, 2048], F32))
        xnf32 = e(nc.sbuf_tensor("xnf32", [128, 2048], F32))
        xnf = e(nc.sbuf_tensor("xnf", [128, 2048], BF16))
        lat = e(nc.sbuf_tensor("lat", [128, 512], BF16))
        qoct = e(nc.sbuf_tensor("qoct", [128, 4096], BF16))
        koct = e(nc.sbuf_tensor("koct", [128, 4096], BF16))
        vdm = e(nc.sbuf_tensor("vdm", [64, 4096], BF16))
        vst = e(nc.sbuf_tensor("vst", [128, 4096], BF16))
        p32 = e(nc.sbuf_tensor("p32", [128, 512], F32))
        pex32 = e(nc.sbuf_tensor("pex32", [128, 512], F32))
        maskb = e(nc.sbuf_tensor("maskb", [128, 32], F32))
        mb1 = e(nc.sbuf_tensor("mb1", [128, 1], F32))
        umat = e(nc.sbuf_tensor("umat", [2, 128], BF16))
        emat = e(nc.sbuf_tensor("emat", [2, 32], BF16))
        mx16 = e(nc.sbuf_tensor("mx16", [128, 16], F32))
        s16 = e(nc.sbuf_tensor("s16", [128, 16], F32))
        r16 = e(nc.sbuf_tensor("r16", [128, 16], F32))
        pdiag = e(nc.sbuf_tensor("pdiag", [128, 2048], BF16))
        pav = e(nc.sbuf_tensor("pav", [128, 2048], BF16))
        o_f = e(nc.sbuf_tensor("o_f", [128, 2048], BF16))
        mrep = e(nc.sbuf_tensor("mrep", [128, 2048], BF16))
        indw_t = e(nc.sbuf_tensor("indw_t", [128, 16], BF16))
        indw_f = e(nc.sbuf_tensor("indw_f", [8, 256], BF16))
        xe = e(nc.sbuf_tensor("xe", [128, 4096], BF16))
        y_f = e(nc.sbuf_tensor("y_f", [128, 2048], F32))
        exbuf = e(nc.sbuf_tensor("exbuf", [128, 16384], BF16))
        outsb = e(nc.sbuf_tensor("outsb", [128, 2048], F32))
        wstage = y_f
        irow = y_f[:, 1024:1152]
        lg = e(nc.sbuf_tensor("lg", [128, 16], F32))
        eqv = e(nc.sbuf_tensor("eqv", [128, 16], F32))
        selv = e(nc.sbuf_tensor("selv", [128, 16], F32))
        idxv = e(nc.sbuf_tensor("idxv", [128, 2], F32))
        mxl = e(nc.sbuf_tensor("mxl", [128, 2], F32))
        nmxl = e(nc.sbuf_tensor("nmxl", [128, 2], F32))
        denl = e(nc.sbuf_tensor("denl", [128, 2], F32))
        wgt = e(nc.sbuf_tensor("wgt", [128, 2], F32))
        expl = e(nc.sbuf_tensor("expl", [128, 16], F32))
        stats = e(nc.sbuf_tensor("stats", [128, 24], F32))
        mv = e(nc.sbuf_tensor("mv", [128, 4], F32))
        rinv = e(nc.sbuf_tensor("rinv", [128, 2], F32))
        lnv = e(nc.sbuf_tensor("lnv", [128, 2], F32))
        negb = e(nc.sbuf_tensor("negb", [128, 2], F32))
        psb = e(nc.psum_tensor("psb", [128, 2048], F32))
        psa = e(nc.psum_tensor("psa", [128, 2048], F32))
        psa_bf = psa[:].bitcast(BF16)

        sx0 = e(nc.semaphore("sx0"))
        sx1 = e(nc.semaphore("sx1"))
        sex0 = e(nc.semaphore("sex0"))
        sex1 = e(nc.semaphore("sex1"))
        sexg0 = e(nc.semaphore("sexg0"))
        sexg1 = e(nc.semaphore("sexg1"))
        sexp = e(nc.semaphore("sexp"))
        sw = e(nc.semaphore("sw"))
        smT = e(nc.semaphore("smT"))
        smS = e(nc.semaphore("smS"))
        smV = e(nc.semaphore("smV"))
        sxv = e(nc.semaphore("sxv"))
        sxu = e(nc.semaphore("sxu"))
        so = e(nc.semaphore("so"))
        smG = e(nc.semaphore("smG"))
        block = e(nc.Block())

        def re(ap_, pat, **kw):
            return ap_.rearrange(pat, **kw)

        @block.gpsimd
        def _(gpsimd):
            gpsimd.iota(icol[:], [[0, 1]], channel_multiplier=1,
                        allow_small_or_imprecise_dtypes=True).then_inc(smG, 1)
            gpsimd.iota(irow, [[1, 128]], channel_multiplier=0,
                        allow_small_or_imprecise_dtypes=True).then_inc(smG, 1)
            gpsimd.iota(iota16[:], [[1, 8]], channel_multiplier=0, base=16,
                        allow_small_or_imprecise_dtypes=True).then_inc(smG, 1)
            gpsimd.iota(y_f[0:2, 1152:1280], [[1, 8], [0, 16]], channel_multiplier=0,
                        allow_small_or_imprecise_dtypes=True).then_inc(smG, 1)
            gpsimd.iota(y_f[0:2, 1280:1408], [[1, 4], [0, 32]], channel_multiplier=0,
                        allow_small_or_imprecise_dtypes=True).then_inc(smG, 1)
            gpsimd.iota(y_f[0:2, 1408:1440], [[1, 2], [0, 16]], channel_multiplier=0,
                        allow_small_or_imprecise_dtypes=True).then_inc(smG, 1)
            gpsimd.nop().then_inc(smV, 1)

            def gdma(dst, src, sem):
                gpsimd.dma_start(out=dst, in_=src).then_inc(sem, 16)

            def xload(j):
                xo = (j % 2) * 2048
                sxi = sx0 if j % 2 == 0 else sx1
                gdma(x_t[:, xo:xo + 1024], x_d[TOK * j:TOK * j + 128, :], sxi)
                gdma(x_t[:, xo + 1024:xo + 2048],
                     x_d[TOK * j + 128:TOK * j + 256, :], sxi)

            if not phase0_only:
                xload(0)
                if nt > 1:
                    xload(1)
            for ex in range(8):
                gpsimd.wait_ge(smS, 1 + 8 * (ex + 1))
                if ex > 0:
                    gpsimd.wait_ge(sexp, 16 * ex)
                gdma(exs_d[ex], exbuf[:, 8192:16384], sexp)
            if not phase0_only:
                for i in range(nt):
                    for ex in range(8):
                        n = 8 * i + ex
                        if n == 0:
                            gpsimd.wait_ge(smT, P0_T)
                            gpsimd.wait_ge(sexp, P0_SEX)
                        if n >= 2:
                            gpsimd.wait_ge(sxu, n - 1)
                        gdma(exbuf[:, 8192 * (n % 2) + 4096:8192 * (n % 2) + 8192],
                             exs_d[ex][:, 4096:8192],
                             sexg0 if n % 2 == 0 else sexg1)
                    gpsimd.wait_ge(smV, vv(i, 'yadd'))
                    if i + 2 < nt:
                        xload(i + 2)
                    for g in range(2):
                        gdma(out_d[TOK * i + 128 * g:TOK * i + 128 * (g + 1), :],
                             outsb[:, 1024 * g:1024 * (g + 1)], so)

        @block.vector
        def _(vector):
            cV = [1]

            def vinc(ins=None):
                if ins is None:
                    ins = vector.drain()
                ins.then_inc(smV, 1)
                cV[0] += 1

            vector.wait_ge(smG, 6)
            vector.tensor_scalar(out=id32[:], in0=irow, scalar1=icol[:, 0:1],
                                 scalar2=None, op0=ALU.is_equal)
            vector.drain()
            vector.tensor_copy(out=idbf[:], in_=id32[:])
            vector.memset(qoct[:], 0.0)
            vector.memset(pdiag[:], 0.0)
            vector.memset(epsc[:], 1e-5)
            vector.memset(ones_b[:], 1.0)
            vector.memset(ones_f[:], 1.0)
            tmpi = irow[0:8, 0:8].unsqueeze(2).broadcast_to([8, 8, 128])
            vector.tensor_scalar(out=re(unit[:, :], "p (ee c) -> p ee c", c=128),
                                 in0=tmpi, scalar1=icol[0:8, 0:1],
                                 scalar2=None, op0=ALU.is_equal)
            vector.memset(mb1[:], -1.0e9)
            vector.scalar_tensor_tensor(out=y_f[0:2, 1440:1568], in0=y_f[0:2, 1280:1408],
                                        scalar=-2.0, in1=y_f[0:2, 1152:1280],
                                        op0=ALU.mult, op1=ALU.add)
            vector.drain()
            vector.tensor_scalar(out=umat[:], in0=y_f[0:2, 1440:1568],
                                 scalar1=icol[0:2, 0:1], scalar2=None, op0=ALU.is_equal)
            vector.tensor_scalar(out=emat[:], in0=y_f[0:2, 1408:1440],
                                 scalar1=icol[0:2, 0:1], scalar2=None, op0=ALU.is_equal)
            vector.drain()
            vinc()
            wcount = 0
            for r in range(4):
                wcount += 2
                vector.wait_ge(sw, 16 * wcount)
                vinc(vector.tensor_copy(out=wq[:, 2048 * r:2048 * (r + 1)], in_=wstage[:]))
            wcount += 8
            vector.wait_ge(sw, 16 * wcount)
            vinc(vector.tensor_copy(out=wdn[:], in_=wstage[:]))
            for kc in range(2):
                wcount += 1
                vector.wait_ge(sw, 16 * wcount)
                dk = re(wup[:, 4096 * kc:4096 * kc + 2048], "p (h half d) -> p h half d",
                        half=2, d=64)
                sk = re(wstage[:, :], "p (h x) -> p h x", x=128).unsqueeze(2)[:, :, :, 0:64]
                vector.tensor_copy(out=dk, in_=sk.broadcast_to([128, 16, 2, 64]))
                dvv = re(wup[:, 4096 * kc + 2048:4096 * kc + 4096],
                         "p (h half d) -> p h half d", half=2, d=64)
                svv = re(wstage[:, :], "p (h x) -> p h x", x=128).unsqueeze(2)[:, :, :, 64:128]
                vinc(vector.tensor_copy(out=dvv, in_=svv.broadcast_to([128, 16, 2, 64])))
            for r in range(4):
                wcount += 2
                vector.wait_ge(sw, 16 * wcount)
                vinc(vector.tensor_copy(out=wo[:, 2048 * r:2048 * (r + 1)], in_=wstage[:]))
            wcount += 1
            vector.wait_ge(sw, 16 * wcount)
            vinc()
            vector.wait_ge(smT, 1)
            vinc(vector.tensor_copy(out=rT[:], in_=psa[:, 0:64]))
            vector.memset(wstage[0:16, 0:128], 0.0)
            vinc()
            wcount += 5
            vector.wait_ge(smT, 2)
            srcq0 = re(psa[:, 80:96], "p (par m) -> p par m", m=8)
            dstq0 = re(qpatb[:], "p (m par) -> p par m", par=2)
            vector.tensor_scalar(out=dstq0, in0=srcq0, scalar1=0.125, scalar2=None,
                                 op0=ALU.mult)
            vector.tensor_copy(out=kpatb[:], in_=psa[:, 96:112])
            vector.tensor_copy(out=vpatb[:], in_=psa[0:64, 112:128])
            vinc()
            wcount += 3
            vector.wait_ge(sw, 16 * wcount)
            vinc(vector.tensor_copy(out=bo_r[:], in_=wstage[0:1, 0:1024]))
            vinc()
            vector.wait_ge(smT, 3)
            vinc(vector.tensor_copy(out=rbf[:], in_=psa[:, 64:72]))
            wcount += 1
            vector.wait_ge(sw, 16 * wcount)
            vinc(vector.tensor_copy(out=exb[:], in_=wstage[0:8, 0:1024]))
            for ex in range(8):
                for sr in range(4):
                    wcount += 2
                    vector.wait_ge(sw, 16 * wcount)
                    if sr == 0 and ex > 0:
                        vector.wait_ge(smT, 4 + 8 * ex)
                    vinc(vector.tensor_copy(out=exbuf[:, 2048 * sr:2048 * (sr + 1)],
                                            in_=wstage[:]))
            assert cV[0] == P0_V - 2, cV[0]
            if phase0_only:
                return

            def ln_front(j):
                xo2 = (j % 2) * 2048
                vector.wait_ge(sx0 if j % 2 == 0 else sx1, 32 * (j // 2 + 1))
                for g in range(2):
                    vector.bn_stats(out=stats[:, 12 * g:12 * g + 6],
                                    in_=x_t[:, xo2 + 1024 * g:xo2 + 1024 * g + 512])
                    vector.bn_stats(out=stats[:, 12 * g + 6:12 * g + 12],
                                    in_=x_t[:, xo2 + 1024 * g + 512:
                                            xo2 + 1024 * g + 1024])
                vector.drain()
                for g in range(2):
                    vector.bn_aggr(out=mv[:, 2 * g:2 * g + 2],
                                   in_=stats[:, 12 * g:12 * g + 12])
                vector.drain()

            def negb_calc():
                vector.scalar_tensor_tensor(
                    out=negb[:], in0=re(mv[:], "p (g two) -> p g two", two=2)[:, :, 0],
                    scalar=-1.0, in1=rinv[:], op0=ALU.mult, op1=ALU.mult)
                vector.drain()

            ln_front(0)
            vinc()
            vector.wait_ge(smS, P0_S)
            negb_calc()
            vinc()
            for i in range(nt):
                xo = (i % 2) * 2048
                vector.wait_ge(smT, tv(i, 'xnT'))
                if i > 0:
                    vector.wait_ge(smT, tv(i - 1, 'lat'))
                vector.tensor_copy(out=xnf32[:], in_=psb[:])
                dst2 = re(re(xnf[:], "p (cc t) -> p cc t", t=256),
                          "p cc (g x) -> p g cc x", g=2)
                src2 = re(psb[:], "p (g cc x) -> p g cc x", cc=8, x=128)
                vector.tensor_copy(out=dst2, in_=src2)
                vinc()
                vector.wait_ge(smT, tv(i, 'router'))
                vector.tensor_copy(out=lg[:, 0:8], in_=psa[:, 512:520])
                vector.tensor_copy(out=lg[:, 8:16], in_=psa[:, 1024:1032])
                vector.drain()
                src_rb = rbf[:].unsqueeze(1).broadcast_to([128, 2, 8])
                vector.tensor_tensor(out=re(lg[:], "p (g ee) -> p g ee", ee=8),
                                     in0=re(lg[:], "p (g ee) -> p g ee", ee=8),
                                     in1=src_rb, op=ALU.add)
                vector.drain()
                vector.tensor_reduce(out=mxl[:], in_=re(lg[:], "p (g ee) -> p g ee", ee=8),
                                     axis=AX.X, op=ALU.max)
                vector.drain()
                vector.tensor_scalar_mul(out=nmxl[:], in0=mxl[:], scalar1=-1.0)
                vector.drain()
                vinc()
                for g in range(2):
                    vector.tensor_scalar(out=eqv[:, 8 * g:8 * g + 8],
                                         in0=lg[:, 8 * g:8 * g + 8],
                                         scalar1=mxl[:, g:g + 1], scalar2=None,
                                         op0=ALU.is_ge)
                vector.drain()
                for g in range(2):
                    vector.scalar_tensor_tensor(out=selv[:, 8 * g:8 * g + 8],
                                                in0=eqv[:, 8 * g:8 * g + 8], scalar=-16.0,
                                                in1=iota16[:], op0=ALU.mult, op1=ALU.add)
                vector.drain()
                vector.tensor_reduce(out=idxv[:], in_=re(selv[:], "p (g ee) -> p g ee", ee=8),
                                     axis=AX.X, op=ALU.min)
                vector.drain()
                vector.wait_ge(smS, sv(i, 'den'))
                vector.reciprocal(out=wgt[:], in_=denl[:])
                vector.drain()
                for g in range(2):
                    vector.tensor_scalar(out=eqv[:, 8 * g:8 * g + 8], in0=iota16[:],
                                         scalar1=idxv[:, g:g + 1], scalar2=16.0,
                                         op0=ALU.subtract, op1=ALU.is_equal)
                vector.drain()
                for g in range(2):
                    vector.tensor_scalar(out=indw_t[:, 8 * g:8 * g + 8],
                                         in0=eqv[:, 8 * g:8 * g + 8],
                                         scalar1=wgt[:, g:g + 1], scalar2=None,
                                         op0=ALU.mult)
                vector.drain()
                vinc()
                vector.wait_ge(smT, tv(i, 'prepA'))
                vinc(vector.tensor_copy(out=indw_f[:], in_=psa_bf[0:8, 3072:3328]))
                vector.wait_ge(smT, tv(i, 'prepB'))
                vector.tensor_copy(out=mrep[:], in_=psa[:])
                vector.drain()
                vinc()
                vector.wait_ge(smS, sv(i, 'qev1'))
                vector.wait_ge(smS, sv(i, 'kv3ev'))
                q3 = re(qoct[:], "p (oj h) -> p oj h", h=16)
                vector.tensor_tensor(out=q3, in0=q3,
                                     in1=qpatb[:].unsqueeze(1).broadcast_to([128, 256, 16]),
                                     op=ALU.add)
                k3 = re(koct[:], "p (oj g) -> p oj g", g=16)
                vector.tensor_tensor(out=k3, in0=k3,
                                     in1=kpatb[:].unsqueeze(1).broadcast_to([128, 256, 16]),
                                     op=ALU.add)
                vector.drain()
                vinc()
                vector.wait_ge(smS, sv(i, 'kv7ev'))
                for w in range(2):
                    v3 = re(vdm[:, 2048 * w:2048 * (w + 1)], "p (oj g) -> p oj g", g=16)
                    vector.tensor_tensor(
                        out=v3, in0=v3,
                        in1=vpatb[:].unsqueeze(1).broadcast_to([64, 128, 16]),
                        op=ALU.add)
                    vector.drain()
                    vinc()
                for ex_i in range(2):
                    n = 8 * i + ex_i
                    if n >= 2:
                        vector.wait_ge(sxu, n - 1)
                    src0 = re(xnf[:], "p (cc t) -> p cc t", t=256)
                    m1 = mrep[:, 256 * ex_i:256 * (ex_i + 1)].unsqueeze(1)
                    vector.tensor_tensor(
                        out=re(xe[:, 2048 * (ex_i % 2):2048 * (ex_i % 2) + 2048],
                               "p (cc t) -> p cc t", t=256),
                        in0=src0, in1=m1.broadcast_to([128, 8, 256]),
                        op=ALU.mult).then_inc(sxv, 1)
                    vinc()
                for w in range(2):
                    vector.wait_ge(smS, sv(i, 'extr%d' % w))
                    mb3 = maskb[:].unsqueeze(1).broadcast_to([128, 16, 32])
                    vector.tensor_tensor(out=re(p32[:], "p (o c) -> p o c", c=32),
                                         in0=re(p32[:], "p (o c) -> p o c", c=32),
                                         in1=mb3, op=ALU.add)
                    vector.drain()
                    vector.tensor_reduce(out=mx16[:],
                                         in_=re(p32[:], "p (o c) -> p o c", c=32),
                                         axis=AX.X, op=ALU.max)
                    vector.drain()
                    m3 = mx16[:].unsqueeze(2).broadcast_to([128, 16, 32])
                    vector.tensor_tensor(out=re(p32[:], "p (o c) -> p o c", c=32),
                                         in0=re(p32[:], "p (o c) -> p o c", c=32),
                                         in1=m3, op=ALU.subtract)
                    vector.drain()
                    vinc()
                    vector.wait_ge(smS, sv(i, 'exp%d' % w))
                    vector.tensor_reduce(out=s16[:],
                                         in_=re(pex32[:], "p (o c) -> p o c", c=32),
                                         axis=AX.X, op=ALU.add)
                    vector.drain()
                    vector.reciprocal(out=r16[:], in_=s16[:])
                    vector.drain()
                    for a in range(4):
                        srcn = re(pex32[32 * a:32 * a + 32, :], "p (o c) -> p o c", c=32)
                        r3 = r16[32 * a:32 * a + 32, :].unsqueeze(2)
                        dstn = re(pdiag[32 * a:32 * a + 32, :], "p (o x) -> p o x",
                                  x=128)[:, :, 32 * a:32 * a + 32]
                        vector.tensor_tensor(out=dstn, in0=srcn,
                                             in1=r3.broadcast_to([32, 16, 32]),
                                             op=ALU.mult)
                    vector.drain()
                    vinc()
                    if w == 1:
                        vector.wait_ge(smT, tv(i, 'av0'))
                    elif i > 0:
                        vector.wait_ge(smT, tv(i - 1, 'av1'))
                    vector.transpose(out=pav[:], in_=pdiag[:])
                    vector.drain()
                    vinc()
                for ex_i in range(2, 8):
                    n = 8 * i + ex_i
                    if n >= 2:
                        vector.wait_ge(sxu, n - 1)
                    src0 = re(xnf[:], "p (cc t) -> p cc t", t=256)
                    m1 = mrep[:, 256 * ex_i:256 * (ex_i + 1)].unsqueeze(1)
                    vector.tensor_tensor(
                        out=re(xe[:, 2048 * (ex_i % 2):2048 * (ex_i % 2) + 2048],
                               "p (cc t) -> p cc t", t=256),
                        in0=src0, in1=m1.broadcast_to([128, 8, 256]),
                        op=ALU.mult).then_inc(sxv, 1)
                if i + 1 < nt:
                    ln_front(i + 1)
                vinc()
                if i + 1 < nt:
                    vector.wait_ge(smS, sv(i, 'nrinv'))
                    vector.wait_ge(smS, sv(i, 'apply'))
                    negb_calc()
                vinc()
                vector.wait_ge(smT, tv(i, 'wout'))
                if i > 0:
                    vector.wait_ge(smT, tv(i - 1, 'yT'))
                vector.tensor_copy(out=y_f[:], in_=psb[:])
                vector.drain()
                vinc()
                vector.wait_ge(smT, tv(i, 'yT'))
                if i > 0:
                    vector.wait_ge(so, 32 * i)
                for g in range(2):
                    vector.tensor_tensor(out=outsb[:, 1024 * g:1024 * (g + 1)],
                                         in0=psa[:, 1024 * g:1024 * (g + 1)],
                                         in1=x_t[:, xo + 1024 * g:xo + 1024 * (g + 1)],
                                         op=ALU.add)
                vector.drain()
                vinc()
            assert cV[0] == P0_V + nt * V_PER, cV[0]

        @block.scalar
        def _(scalar):
            cS = [0]

            def sinc(ins=None):
                if ins is None:
                    ins = scalar.drain()
                ins.then_inc(smS, 1)
                cS[0] += 1

            scalar.wait_ge(smT, 4)
            sinc(scalar.activation(out=maskb[:], in_=psa[:, 128:160], func=AF.Identity,
                                   scale=1.0e9, bias=mb1[:, 0:1]))
            for ex in range(8):
                for kd in range(8):
                    scalar.wait_ge(smT, 4 + 8 * ex + kd + 1)
                    if kd == 0 and ex > 0:
                        scalar.wait_ge(sexp, 16 * ex)
                    sinc(scalar.copy(
                        out=exbuf[:, 8192 + 1024 * kd:8192 + 1024 * (kd + 1)],
                        in_=psa_bf[:, 1024 * (kd % 4):1024 * (kd % 4) + 1024]))
            assert cS[0] == P0_S - 1, cS[0]
            if phase0_only:
                return

            def rinv_calc():
                scalar.activation(out=lnv[:],
                                  in_=re(mv[:], "p (g two) -> p g two", two=2)[:, :, 1],
                                  func=AF.Ln, bias=epsc[:, 0:1])
                scalar.drain()
                scalar.activation(out=rinv[:], in_=lnv[:], func=AF.Exp, scale=-0.5)
                scalar.drain()

            scalar.wait_ge(smV, P0_V - 1)
            rinv_calc()
            sinc()
            for i in range(nt):
                xo = (i % 2) * 2048
                scalar.wait_ge(smV, vv(i - 1, 'nnegb') if i > 0 else P0_V)
                if i > 0:
                    scalar.wait_ge(smT, tv(i - 1, 'xnT'))
                for g in range(2):
                    scalar.activation(out=xn_t[:, 1024 * g:1024 * (g + 1)],
                                      in_=x_t[:, xo + 1024 * g:xo + 1024 * (g + 1)],
                                      func=AF.Identity, scale=rinv[:, g:g + 1],
                                      bias=negb[:, g:g + 1])
                sinc()
                scalar.wait_ge(smV, vv(i, 'lgev'))
                for g in range(2):
                    scalar.activation(out=expl[:, 8 * g:8 * g + 8],
                                      in_=lg[:, 8 * g:8 * g + 8],
                                      func=AF.Exp, bias=nmxl[:, g:g + 1],
                                      accum_out=denl[:, g:g + 1])
                sinc()
                for mh in range(2):
                    scalar.wait_ge(smT, tv(i, 'q%d' % mh))
                    if mh == 0 and i > 0:
                        scalar.wait_ge(smT, tv(i - 1, 'qk1'))
                    for hl in range(2):
                        srcq = re(psb[64 * hl:64 * hl + 64, 1024 * mh:1024 * (mh + 1)],
                                  "p (cc og j) -> p og j cc", og=32, j=8)
                        dstq = re(qoct[64 * hl:64 * hl + 64, :],
                                  "p (og j cc two) -> p og j cc two", j=8, cc=8,
                                  two=2)[:, :, :, 4 * mh:4 * mh + 4, hl]
                        scalar.activation(out=dstq, in_=srcq, func=AF.Identity,
                                          scale=0.125)
                    sinc()
                scalar.wait_ge(smT, tv(i, 'lat'))
                for kc in range(2):
                    scalar.activation(out=lat[:, 256 * kc:256 * (kc + 1)],
                                      in_=psa[:, 256 * kc:256 * (kc + 1)],
                                      func=AF.Identity, bias=bdn_p[:, kc:kc + 1])
                sinc()
                for r in range(8):
                    scalar.wait_ge(smT, tv(i, 'kv%d' % r))
                    if r == 0 and i > 0:
                        scalar.wait_ge(smT, tv(i - 1, 'qk1'))
                    if r == 4 and i > 0:
                        scalar.wait_ge(smT, tv(i - 1, 'vT1'))
                    hb = 1024 * (r % 2)
                    if r < 4:
                        srck = re(psb[:, hb:hb + 1024], "p (g og j) -> p og j g",
                                  og=32, j=8)
                        dstk = re(koct[:], "p (og j gg) -> p og j gg", j=8,
                                  gg=16)[:, :, :, 4 * r:4 * r + 4]
                        scalar.copy(out=dstk, in_=srck)
                    else:
                        srcv = re(psb[0:64, hb:hb + 1024], "p (g og j) -> p og j g",
                                  og=32, j=8)
                        dstv = re(vdm[:], "p (og j gg) -> p og j gg", j=8,
                                  gg=16)[:, :, :, 4 * (r - 4):4 * (r - 4) + 4]
                        scalar.copy(out=dstv, in_=srcv)
                    sinc()
                scalar.wait_ge(smT, tv(i, 'vT0'))
                if i > 0:
                    scalar.wait_ge(smT, tv(i - 1, 'av0'))
                srcv = re(psa_bf[:, 0:1024], "p (o d) -> p o d", d=64).unsqueeze(2)
                dstv = re(vst[:, 0:2048], "p (o hl d) -> p o hl d", hl=2, d=64)
                scalar.copy(out=dstv, in_=srcv.broadcast_to([128, 16, 2, 64]))
                sinc()
                scalar.wait_ge(smT, tv(i, 'qk0'))
                for a in range(4):
                    srca = re(psa[32 * a:32 * a + 32, :], "p (o x) -> p o x",
                              x=128)[:, 0:4, 32 * a:32 * a + 32]
                    dsta = re(p32[32 * a:32 * a + 32, :], "p (o c) -> p o c",
                              c=32)[:, 0:4]
                    scalar.copy(out=dsta, in_=srca)
                sinc()
                for a in range(4):
                    srca = re(psa[32 * a:32 * a + 32, :], "p (o x) -> p o x",
                              x=128)[:, 4:16, 32 * a:32 * a + 32]
                    dsta = re(p32[32 * a:32 * a + 32, :], "p (o c) -> p o c",
                              c=32)[:, 4:16]
                    scalar.copy(out=dsta, in_=srca)
                sinc()
                scalar.wait_ge(smT, tv(i, 'vT1'))
                if i > 0:
                    scalar.wait_ge(smT, tv(i - 1, 'av1'))
                srcv = re(psa_bf[:, 0:1024], "p (o d) -> p o d", d=64).unsqueeze(2)
                dstv = re(vst[:, 2048:4096], "p (o hl d) -> p o hl d", hl=2, d=64)
                scalar.copy(out=dstv, in_=srcv.broadcast_to([128, 16, 2, 64]))
                sinc()
                scalar.wait_ge(smV, vv(i, 'sm0'))
                scalar.activation(out=pex32[:], in_=p32[:], func=AF.Exp)
                sinc(scalar.drain())
                scalar.wait_ge(smT, tv(i, 'qk1'))
                for a in range(4):
                    srca = re(psa[32 * a:32 * a + 32, :], "p (o x) -> p o x",
                              x=128)[:, :, 32 * a:32 * a + 32]
                    dsta = re(p32[32 * a:32 * a + 32, :], "p (o c) -> p o c", c=32)
                    scalar.copy(out=dsta, in_=srca)
                sinc()
                scalar.wait_ge(smV, vv(i, 'sm1'))
                scalar.wait_ge(smV, vv(i, 'sct0'))
                scalar.activation(out=pex32[:], in_=p32[:], func=AF.Exp)
                sinc(scalar.drain())
                scalar.wait_ge(smT, tv(i, 'av0'))
                if i > 0:
                    scalar.wait_ge(smT, tv(i - 1, 'wout'))
                for hl in range(2):
                    for a in range(4):
                        srco = re(psa[64 * hl:64 * hl + 64, :],
                                  "p (aa ol jl cc two) -> p two aa cc ol jl",
                                  ol=16, jl=2, cc=8, two=2)[:, hl, a]
                        dsto = re(o_f[64 * hl:64 * hl + 64, :],
                                  "p (cc ww ol aa jl) -> p ww aa cc ol jl",
                                  ww=2, ol=16, aa=4, jl=2)[:, 0, a]
                        scalar.copy(out=dsto, in_=srco)
                sinc()
                scalar.wait_ge(smT, tv(i, 'av1'))
                for hl in range(2):
                    for a in range(4):
                        srco = re(psa[64 * hl:64 * hl + 64, :],
                                  "p (aa ol jl cc two) -> p two aa cc ol jl",
                                  ol=16, jl=2, cc=8, two=2)[:, hl, a]
                        dsto = re(o_f[64 * hl:64 * hl + 64, :],
                                  "p (cc ww ol aa jl) -> p ww aa cc ol jl",
                                  ww=2, ol=16, aa=4, jl=2)[:, 1, a]
                        scalar.copy(out=dsto, in_=srco)
                sinc()
                if i + 1 < nt:
                    scalar.wait_ge(smV, vv(i, 'nstats'))
                    rinv_calc()
                sinc()
            assert cS[0] == P0_S + nt * S_PER, cS[0]

        @block.tensor
        def _(tensor):
            cT = [0]

            def tinc(ins):
                ins.then_inc(smT, 1)
                cT[0] += 1

            tensor.wait_ge(smV, 14)
            li = None
            for cc in range(8):
                li = tensor.transpose(out=psa[:, 8 * cc:8 * cc + 8],
                                      in_=wstage[0:8, 128 * cc:128 * (cc + 1)],
                                      identity=id32[0:8, 0:8])
            tinc(li)
            tensor.wait_ge(smV, 16)
            tensor.wait_ge(sw, 16 * 32)
            tensor.transpose(out=psa[:, 80:96], in_=wstage[0:16, 0:128],
                             identity=id32[0:16, 0:16])
            tensor.transpose(out=psa[:, 96:112], in_=wstage[0:16, 128:256],
                             identity=id32[0:16, 0:16])
            li = tensor.transpose(out=psa[0:64, 112:128], in_=wstage[0:16, 256:320],
                                  identity=id32[0:16, 0:16])
            tinc(li)
            tensor.wait_ge(smV, 19)
            li = tensor.matmul(out=psa[:, 64:72], lhsT=ones_f[:],
                               rhs=wstage[0:1, 1024:1032], start=True, stop=True)
            tinc(li)
            tensor.wait_ge(smV, 20)
            li = tensor.matmul(out=psa[:, 128:160], lhsT=umat[:], rhs=emat[:],
                               start=True, stop=True)
            tinc(li)
            tensor.wait_ge(smS, 1)
            for ex in range(8):
                tensor.wait_ge(smV, 21 + 4 * (ex + 1))
                for kd in range(8):
                    if 8 * ex + kd >= 2:
                        tensor.wait_ge(smS, 8 * ex + kd - 1)
                    for jo in range(8):
                        li = tensor.transpose(
                            out=psa_bf[:, 1024 * (kd % 4) + 128 * jo:
                                       1024 * (kd % 4) + 128 * (jo + 1)],
                            in_=exbuf[:, 1024 * jo + 128 * kd:1024 * jo + 128 * (kd + 1)],
                            identity=idbf[:])
                    tinc(li)
            assert cT[0] == P0_T, cT[0]
            if phase0_only:
                return
            for i in range(nt):
                tensor.wait_ge(smS, sv(i, 'apply'))
                if i > 0:
                    tensor.wait_ge(smV, vv(i - 1, 'yev'))
                for g in range(2):
                    for cc in range(8):
                        li = tensor.transpose(
                            out=psb[:, 128 * (8 * g + cc):128 * (8 * g + cc + 1)],
                            in_=xn_t[:, 1024 * g + 128 * cc:1024 * g + 128 * (cc + 1)],
                            identity=id32[:])
                tinc(li)
                tensor.wait_ge(smV, vv(i, 'xnev'))
                if i > 0:
                    tensor.wait_ge(smV, vv(i - 1, 'yadd'))
                for g in range(2):
                    off = 512 * (g + 1)
                    for cc in range(8):
                        li = tensor.matmul(out=psa[:, off:off + 8],
                                           lhsT=xnf32[:, 128 * (8 * g + cc):
                                                      128 * (8 * g + cc + 1)],
                                           rhs=rT[:, 8 * cc:8 * cc + 8],
                                           start=(cc == 0), stop=(cc == 7))
                tinc(li)
                for mh in range(2):
                    for m in range(4 * mh, 4 * mh + 4):
                        for k in range(8):
                            li = tensor.matmul(out=psb[:, 256 * m:256 * (m + 1)],
                                               lhsT=wq[:, 1024 * k + 128 * m:
                                                       1024 * k + 128 * (m + 1)],
                                               rhs=xnf[:, 256 * k:256 * (k + 1)],
                                               start=(k == 0), stop=(k == 7))
                    tinc(li)
                tensor.wait_ge(smV, vv(i, 'router'))
                for g in range(2):
                    li = tensor.transpose(out=psa_bf[0:8, 3072 + 128 * g:
                                                     3072 + 128 * (g + 1)],
                                          in_=indw_t[:, 8 * g:8 * g + 8],
                                          identity=idbf[:])
                tinc(li)
                tensor.wait_ge(smV, vv(i, 'indwev'))
                tensor.wait_ge(smV, vv(i, 'lgev'))
                for ex in range(8):
                    li = tensor.matmul(out=psa[:, 256 * ex:256 * (ex + 1)],
                                       lhsT=unit[:, 128 * ex:128 * (ex + 1)],
                                       rhs=indw_f[:], start=True, stop=True)
                tinc(li)
                tensor.wait_ge(smV, vv(i, 'mrep'))
                for m in range(2):
                    for k in range(8):
                        li = tensor.matmul(out=psa[:, 256 * m:256 * (m + 1)],
                                           lhsT=wdn[:, 256 * k + 128 * m:
                                                    256 * k + 128 * (m + 1)],
                                           rhs=xnf[:, 256 * k:256 * (k + 1)],
                                           start=(k == 0), stop=(k == 7))
                tinc(li)
                for r in range(8):
                    if r == 0:
                        tensor.wait_ge(smS, sv(i, 'latev'))
                        tensor.wait_ge(smS, sv(i, 'qev0'))
                    elif r == 1:
                        tensor.wait_ge(smS, sv(i, 'qev1'))
                    elif r >= 2:
                        tensor.wait_ge(smS, sv(i, 'kv%dev' % (r - 2)))
                    hb = 1024 * (r % 2)
                    for gch in range(4):
                        ch = 4 * r + gch
                        for k in range(2):
                            li = tensor.matmul(out=psb[:, hb + 256 * gch:
                                                       hb + 256 * (gch + 1)],
                                               lhsT=wup[:, 4096 * k + 128 * ch:
                                                        4096 * k + 128 * (ch + 1)],
                                               rhs=lat[:, 256 * k:256 * (k + 1)],
                                               start=(k == 0), stop=(k == 1))
                    tinc(li)
                tensor.wait_ge(smV, vv(i, 'vb0'))
                tensor.wait_ge(smS, sv(i, 'latev'))
                tensor.wait_ge(smV, vv(i, 'mrep'))
                for ol in range(16):
                    li = tensor.transpose(out=psa_bf[:, 64 * ol:64 * (ol + 1)],
                                          in_=vdm[:, 128 * ol:128 * (ol + 1)],
                                          identity=idbf[0:64, 0:64])
                tinc(li)
                tensor.wait_ge(smV, vv(i, 'qkb'))
                for ol in range(4, 16):
                    li = tensor.matmul(out=psa[:, 128 * ol:128 * (ol + 1)],
                                       lhsT=qoct[:, 128 * ol:128 * (ol + 1)],
                                       rhs=koct[:, 128 * ol:128 * (ol + 1)],
                                       start=True, stop=True)
                tinc(li)
                tensor.wait_ge(smS, sv(i, 'vst0'))
                for ol in range(4):
                    li = tensor.matmul(out=psa[:, 128 * ol:128 * (ol + 1)],
                                       lhsT=qoct[:, 128 * ol:128 * (ol + 1)],
                                       rhs=koct[:, 128 * ol:128 * (ol + 1)],
                                       start=True, stop=True)
                tinc(li)
                tensor.wait_ge(smS, sv(i, 'extr0a'))
                tensor.wait_ge(smV, vv(i, 'vb1'))
                for ol in range(16):
                    og = 16 + ol
                    li = tensor.transpose(out=psa_bf[:, 64 * ol:64 * (ol + 1)],
                                          in_=vdm[:, 128 * og:128 * (og + 1)],
                                          identity=idbf[0:64, 0:64])
                tinc(li)
                tensor.wait_ge(smS, sv(i, 'extr0'))
                for ol in range(4, 16):
                    og = 16 + ol
                    li = tensor.matmul(out=psa[:, 128 * ol:128 * (ol + 1)],
                                       lhsT=qoct[:, 128 * og:128 * (og + 1)],
                                       rhs=koct[:, 128 * og:128 * (og + 1)],
                                       start=True, stop=True)
                tinc(li)
                tensor.wait_ge(smS, sv(i, 'vst1'))
                for ol in range(4):
                    og = 16 + ol
                    li = tensor.matmul(out=psa[:, 128 * ol:128 * (ol + 1)],
                                       lhsT=qoct[:, 128 * og:128 * (og + 1)],
                                       rhs=koct[:, 128 * og:128 * (og + 1)],
                                       start=True, stop=True)
                tinc(li)
                def pout(m):
                    return psb[:, 256 * m:256 * (m + 1)]

                def expert_mms(ex):
                    n = 8 * i + ex
                    tensor.wait_ge(sxv, n + 1)
                    tensor.wait_ge(sex0 if n % 2 == 0 else sex1, 16 * (n // 2 + 1))
                    tensor.wait_ge(sexg0 if n % 2 == 0 else sexg1, 16 * (n // 2 + 1))
                    eb = 8192 * (n % 2)
                    xb = 2048 * (ex % 2)
                    for m in range(8):
                        for k in range(8):
                            li = tensor.matmul(out=pout(m),
                                               lhsT=exbuf[:, eb + 1024 * k + 128 * m:
                                                           eb + 1024 * k + 128 * (m + 1)],
                                               rhs=xe[:, xb + 256 * k:xb + 256 * (k + 1)],
                                               start=False, stop=False)
                    li.then_inc(sxu, 1)
                    return li

                tensor.wait_ge(smS, sv(i, 'kv7ev'))
                for b in range(4):
                    tensor.matmul(out=psb[:, 512 * b:512 * (b + 1)],
                                  lhsT=unit[0:1, 128:256], rhs=unit[0:1, 0:512],
                                  start=True, stop=False)
                expert_mms(0)
                expert_mms(1)
                tensor.wait_ge(smV, vv(i, 'pav0'))
                tensor.wait_ge(smS, sv(i, 'extr1'))
                for ol in range(16):
                    for a in range(4):
                        li = tensor.matmul(
                            out=psa[:, 512 * a + 32 * ol:512 * a + 32 * (ol + 1)],
                            lhsT=vst[32 * a:32 * a + 32, 128 * ol:128 * (ol + 1)],
                            rhs=pav[32 * a:32 * a + 32,
                                    128 * ol + 32 * a:128 * ol + 32 * a + 32],
                            start=True, stop=True,
                            tile_position=(32 * a, 0))
                tinc(li)
                expert_mms(2)
                expert_mms(3)
                tensor.wait_ge(smV, vv(i, 'pav1'))
                tensor.wait_ge(smS, sv(i, 'of0'))
                for ol in range(16):
                    for a in range(4):
                        li = tensor.matmul(
                            out=psa[:, 512 * a + 32 * ol:512 * a + 32 * (ol + 1)],
                            lhsT=vst[32 * a:32 * a + 32,
                                     2048 + 128 * ol:2048 + 128 * (ol + 1)],
                            rhs=pav[32 * a:32 * a + 32,
                                    128 * ol + 32 * a:128 * ol + 32 * a + 32],
                            start=True, stop=True,
                            tile_position=(32 * a, 0))
                tinc(li)
                for ex in range(4, 8):
                    expert_mms(ex)
                tensor.wait_ge(smS, sv(i, 'of1'))
                for m in range(8):
                    li = tensor.matmul(out=pout(m),
                                       lhsT=exb[:, 128 * m:128 * (m + 1)],
                                       rhs=indw_f[:], start=False, stop=False)
                    if m == 0:
                        tinc(li)
                    for k in range(8):
                        tensor.matmul(out=pout(m),
                                      lhsT=wo[:, 1024 * k + 128 * m:
                                              1024 * k + 128 * (m + 1)],
                                      rhs=o_f[:, 256 * k:256 * (k + 1)],
                                      start=False, stop=False)
                    li = tensor.matmul(out=pout(m),
                                       lhsT=bo_r[:, 128 * m:128 * (m + 1)],
                                       rhs=ones_b[:], start=False,
                                       stop=(m % 2 == 1))
                tinc(li)
                tensor.wait_ge(smV, vv(i, 'yev'))
                for g in range(2):
                    for cc in range(8):
                        li = tensor.transpose(out=psa[:, 1024 * g + 128 * cc:
                                                      1024 * g + 128 * (cc + 1)],
                                              in_=y_f[:, 256 * cc + 128 * g:
                                                      256 * cc + 128 * (g + 1)],
                                              identity=id32[:])
                tinc(li)
            assert cT[0] == P0_T + nt * T_PER, cT[0]

        @block.sync
        def _(sync):
            w = [0]

            def dma(dst, src, sem=None, slow=False):
                s = sem if sem is not None else sw
                sync.dma_start(out=dst, in_=src,
                               allow_slow_non_contiguous=slow).then_inc(s, 16)
                if s is sw:
                    w[0] += 1

            vc = 2
            sync.wait_ge(smV, 2)
            for r in range(4):
                if r > 0:
                    sync.wait_ge(smV, vc + r)
                dma(wstage[:, 0:1024], wq_d[128 * 2 * r:128 * (2 * r + 1), :])
                dma(wstage[:, 1024:2048], wq_d[128 * (2 * r + 1):128 * (2 * r + 2), :])
            vc += 4
            sync.wait_ge(smV, vc)
            for k in range(8):
                dma(wstage[:, 256 * k:256 * (k + 1)], wdn_d[128 * k:128 * (k + 1), :])
            vc += 1
            for kc in range(2):
                sync.wait_ge(smV, vc + kc)
                dma(wstage[:], wup_d[128 * kc:128 * (kc + 1), :])
            vc += 2
            for r in range(4):
                sync.wait_ge(smV, vc + r)
                dma(wstage[:, 0:1024], wo_d[128 * 2 * r:128 * (2 * r + 1), :])
                dma(wstage[:, 1024:2048], wo_d[128 * (2 * r + 1):128 * (2 * r + 2), :])
            vc += 4
            sync.wait_ge(smV, vc)
            dma(wstage[0:8, 0:1024], rw_d[:, :])
            vc += 1
            sync.wait_ge(smV, vc + 2)
            dma(wstage[0:8, 0:64],
                re(bq_d, "(m x) -> m x", m=8)[:, 0:64])
            dma(wstage[8:16, 64:128],
                re(bq_d, "(m x) -> m x", m=8)[:, 64:128])
            dma(wstage[0:16, 128:192],
                re(bup_d, "(h x) -> h x", h=16)[:, 0:64])
            dma(wstage[0:16, 192:256],
                re(bup_d, "(h x) -> h x", h=16)[:, 0:64])
            dma(wstage[0:16, 256:320],
                re(bup_d, "(h x) -> h x", h=16)[:, 64:128])
            sync.wait_ge(smT, 2)
            dma(wstage[0:1, 0:1024], re(bo_d, "(a b) -> a b", a=1))
            dma(bdn_p[:], re(bdn_d, "(c p) -> p c", p=128), slow=True)
            dma(wstage[0:1, 1024:1032], re(rb_d, "(a b) -> a b", a=1))
            vc += 6
            sync.wait_ge(smV, vc)
            dma(wstage[0:8, 0:1024], exb_d[:, :])
            vc += 1
            for ex in range(8):
                for sr in range(4):
                    sync.wait_ge(smV, vc + 4 * ex + sr)
                    dma(wstage[:, 0:1024], exw_d[ex, 128 * 2 * sr:128 * (2 * sr + 1), :])
                    dma(wstage[:, 1024:2048],
                        exw_d[ex, 128 * (2 * sr + 1):128 * (2 * sr + 2), :])
            if phase0_only:
                sync.wait_ge(smT, P0_T)
                sync.wait_ge(smS, P0_S - 1)
                sync.wait_ge(smV, P0_V - 2)
                sync.wait_ge(sexp, P0_SEX)
                sync.dma_start(out=out_d[0:128, :],
                               in_=maskb[:].bitcast(F32) if False else outsb[:, 0:1024]
                               ).then_inc(so, 16)
                sync.wait_ge(so, 16)
                return
            for i in range(nt):
                for ex in range(8):
                    n = 8 * i + ex
                    if n == 0:
                        sync.wait_ge(smT, P0_T)
                        sync.wait_ge(sexp, P0_SEX)
                    if n >= 2:
                        sync.wait_ge(sxu, n - 1)
                    dma(exbuf[:, 8192 * (n % 2):8192 * (n % 2) + 4096],
                        exs_d[ex][:, 0:4096],
                        sem=sex0 if n % 2 == 0 else sex1)
            sync.wait_ge(so, 32 * nt)
            if dbg_dump is not None:
                name, rr, ccn = dbg_dump
                src = {'xnf': xnf, 'xnf32': xnf32, 'xn_t': xn_t, 'qoct': qoct,
                       'koct': koct, 'vdm': vdm, 'vst': vst, 'pcomp': pcomp,
                       'pdiag': pdiag, 'pav': pav, 'o_f': o_f, 'mrep': mrep,
                       'lg': lg, 'indw_t': indw_t, 'y_f': y_f, 'lat': lat,
                       'x_t': x_t, 'outsb': outsb, 'indw_f': indw_f, 'xe': xe,
                       'pex': pex, 'mx16': mx16, 'r16': r16, 'wgt': wgt,
                       'idxv': idxv, 'wup_sb': wup, 'exbuf': exbuf}[name]
                sync.dma_start(out=dbg_d, in_=src[0:rr, 0:ccn]).then_inc(so, 16)
                sync.wait_ge(so, 32 * nt + 16)

    return nc


def kernel(L0_attn_x, L0_attn_Wq, L0_attn_bq, L0_attn_Wdown, L0_attn_bdown,
           L0_attn_Wup, L0_attn_bup, L0_attn_Wout, L0_attn_bout,
           L0_ffn_router_w, L0_ffn_router_b, L0_ffn_expert_w, L0_ffn_expert_b,
           _dbg=None):
    nc = build_core(dbg_dump=_dbg)
    in_maps = []
    for c in range(NCORES):
        in_maps.append({
            "x": np.ascontiguousarray(np.asarray(L0_attn_x)[c]),
            "wq": np.asarray(L0_attn_Wq), "bq": np.asarray(L0_attn_bq),
            "wdn": np.asarray(L0_attn_Wdown), "bdn": np.asarray(L0_attn_bdown),
            "wup": np.asarray(L0_attn_Wup), "bup": np.asarray(L0_attn_bup),
            "wo": np.asarray(L0_attn_Wout), "bo": np.asarray(L0_attn_bout),
            "rw": np.asarray(L0_ffn_router_w), "rb": np.asarray(L0_ffn_router_b),
            "exw": np.asarray(L0_ffn_expert_w), "exb": np.asarray(L0_ffn_expert_b),
        })
    res = run_bass_kernel_spmd(nc, in_maps, core_ids=list(range(NCORES)))
    out = np.stack([res.results[c]["out"] for c in range(NCORES)], axis=0)
    if _dbg is not None:
        return out, [res.results[c].get("dbg") for c in range(NCORES)]
    return out
